# version 34
# baseline (speedup 1.0000x reference)
import numpy as np
from contextlib import ExitStack

import concourse.bass as bass
import concourse.mybir as mybir
from concourse.bass_utils import run_bass_kernel_spmd

F32 = mybir.dt.float32
BF16 = mybir.dt.bfloat16
F8 = mybir.dt.float8e4
AF = mybir.ActivationFunctionType
ALU = mybir.AluOpType
AX = mybir.AxisListType.X

D = 1024
HF = 4096
NH = 8
HD = 128
RC = 12582912.0
INV_SQRT_HD = 0.08838834764831845
EPS = 1e-5
QB = 127.0
P02 = 0.19999999999999996

ENGINES = ("tensor", "vector", "scalar", "sync", "gpsimd")


class Sem:
    def __init__(self, nc, name, is_dma=False):
        self.h = nc.alloc_semaphore(name)
        self.n = 0
        self.is_dma = is_dma
        self.name = name


class KB:

    def __init__(self, nc):
        self.nc = nc
        self.streams = {e: [] for e in ENGINES}
        self.wm = {e: {} for e in ENGINES}
        self.sems = []

    def sem(self, name, is_dma=False):
        s = Sem(self.nc, name, is_dma)
        self.sems.append(s)
        return s

    def emit(self, engine, fn, waits=(), inc=None, inc_k=1):
        waits = tuple(w for w in waits if w is not None)
        tok = None
        if inc is not None:
            inc.n += inc_k
            tok = (inc, inc.n)
        self.streams[engine].append((fn, waits, inc, inc_k))
        return tok

    def flush(self, block):
        for e in ENGINES:
            ops = self.streams[e]
            if not ops:
                continue

            def runner(eng, ops=ops, e=e):
                wm = self.wm[e]
                for fn, waits, inc, inc_k in ops:
                    for (s, v) in waits:
                        if wm.get(s, 0) < v:
                            eng.wait_ge(s.h, v)
                            wm[s] = v
                    ins = fn(eng)
                    if inc is not None:
                        ins.then_inc(inc.h, inc_k)
            getattr(block, e)(runner)
            self.streams[e] = []

    def barrier_reached(self):
        for e in ENGINES:
            for s in self.sems:
                if not s.is_dma:
                    self.wm[e][s] = s.n


class Regs:

    def __init__(self):
        self.d = {}

    def waits(self, key):
        return list(self.d.get(key, ()))

    def use(self, key, toks):
        self.d[key] = [t for t in toks if t is not None]


def ap(t, off, dims):
    return bass.AP(t, off, [list(d) for d in dims])


def build_nc(S=4096, debug=False):
    nc = bass.Bass("TRN2", target_bir_lowering=False)
    P1S = 512
    P3S = 256
    N1 = S // P1S
    N3 = S // P3S
    KC = S // 128
    NTB = P1S // 128
    NQB = P3S // 128
    NHC = HF // 128

    def din(name, shape):
        return nc.dram_tensor(name, shape, F32, kind="ExternalInput")

    x_in = din("x", [S, D])
    wq_in = din("wq", [D, D]); bq_in = din("bq", [D])
    wk_in = din("wk", [D, D]); bk_in = din("bk", [D])
    wv_in = din("wv", [D, D]); bv_in = din("bv", [D])
    wo_in = din("wo", [D, D]); bo_in = din("bo", [D])
    lq1_in = din("lq1", [HD]); lk1_in = din("lk1", [HD])
    lq2_in = din("lq2", [HD]); lk2_in = din("lk2", [HD])
    lnw_in = din("lnw", [D]); lnb_in = din("lnb", [D])
    w1_in = din("w1", [HF, D]); b1_in = din("b1", [HF])
    w2_in = din("w2", [D, HF]); b2_in = din("b2", [D])

    y_out = nc.dram_tensor("y", [S, D], F32, kind="ExternalOutput")
    sk = "ExternalOutput" if debug else "Internal"
    xn_sc = nc.dram_tensor("xn_sc", [8, 128, S], BF16, kind=sk)
    q_sc = nc.dram_tensor("q_sc", [NH, 128, S], BF16, kind=sk)
    part_sc = nc.dram_tensor("part_sc", [S, D], F32, kind=sk)

    kb = KB(nc)
    gd = kb.sem("gd", True)
    gd2 = kb.sem("gd2", True)
    sl = kb.sem("sl", True)
    sb1 = kb.sem("sb1", True)
    sb2 = kb.sem("sb2", True)
    ssl = [kb.sem("ss%d" % i, True) for i in range(10)]
    ssa = [kb.sem("sa%d" % i, True) for i in range(4)]
    od_w2 = kb.sem("od_w2", True)
    sx = [kb.sem("sx%d" % i, True) for i in range(2)]
    sw1 = [kb.sem("sw1%d" % i, True) for i in range(3)]
    sw2 = [kb.sem("sw2%d" % i, True) for i in range(4)]
    sxn = kb.sem("sxn", True)
    sbr = kb.sem("sbr", True)
    sq3 = [kb.sem("sq3%d" % i, True) for i in range(2)]
    sx3 = kb.sem("sx3", True)
    sp3 = [kb.sem("sp3%d" % i, True) for i in range(2)]
    spro = kb.sem("spro", True)
    od_w = kb.sem("od_w", True)
    od_xn = kb.sem("od_xn", True)
    od_part = [kb.sem("od_part%d" % i, True) for i in range(2)]
    od_q = [kb.sem("od_q%d" % i, True) for i in range(6)]
    od_y = [kb.sem("od_y%d" % i, True) for i in range(2)]
    st = kb.sem("st")
    sv = kb.sem("sv")
    ss = kb.sem("ss")
    gp = kb.sem("gp")

    ident = nc.alloc_sbuf_tensor("ident", [128, 128], BF16)
    ones_row = nc.alloc_sbuf_tensor("ones_row", [1, 128], BF16)
    ones_row_f = nc.alloc_sbuf_tensor("ones_row_f", [1, 128], F32)
    ones_col_f = nc.alloc_sbuf_tensor("ones_col_f", [128, 1], F32)
    eps_col = nc.alloc_sbuf_tensor("eps_col", [128, 1], F32)
    nlam_b = nc.alloc_sbuf_tensor("nlam_b", [128, 1], F32)
    iws_b = nc.alloc_sbuf_tensor("iws_b", [128, 2], F32)
    dq_b = nc.alloc_sbuf_tensor("dq_b", [128, 2], F32)
    _stk01 = ExitStack()
    b1s_row = _stk01.enter_context(nc.sbuf_tensor("b1s_row", [1, HF], BF16))
    b2s_row = _stk01.enter_context(nc.sbuf_tensor("b2s_row", [1, D], BF16))
    w1res = _stk01.enter_context(nc.sbuf_tensor("w1res", [128, 8 * HF], F8))
    w2res = _stk01.enter_context(nc.sbuf_tensor("w2res", [128, NHC * D], F8))

    with ExitStack() as _stk:
        stage = _stk.enter_context(nc.sbuf_tensor("p0_stage", [128, 8 * D], F32))
        acc = _stk.enter_context(nc.sbuf_tensor("p0_acc", [128, 80], F32))
        tbf = _stk.enter_context(nc.sbuf_tensor("p0_tbf", [128, 2 * HF], BF16))
        ttmp = _stk.enter_context(nc.sbuf_tensor("p0_tmp", [128, 2 * HF], F32))
        lcol = _stk.enter_context(nc.sbuf_tensor("p0_lcol", [128, 4], F32))
        rcc = _stk.enter_context(nc.sbuf_tensor("p0_rcc", [128, 1], F32))
        scr = _stk.enter_context(nc.sbuf_tensor("p0_sc", [1, 12], F32))
        brow = _stk.enter_context(nc.sbuf_tensor("p0_brow", [1, HF + D], F32))
        ps = _stk.enter_context(nc.psum_tensor("p0_ps", [128, 2048], F32))
        ps1 = _stk.enter_context(nc.psum_tensor("p0_ps1", [128, 512], F32))
        blk = _stk.enter_context(nc.Block())
        tid0 = kb.emit("gpsimd", lambda g: g.memset(ident[:, :], 0.0), inc=gp)
        tid = kb.emit("gpsimd", lambda g: g.affine_select(
            out=ident[:, :], in_=ident[:, :], compare_op=ALU.not_equal,
            fill=1.0, base=0, pattern=[[-1, 128]], channel_multiplier=1),
            waits=[tid0], inc=gp)
        tones = kb.emit("vector", lambda v: v.memset(ones_row[:, :], 1.0),
                        inc=sv)
        tonesf = kb.emit("vector", lambda v: v.memset(ones_row_f[:, :], 1.0),
                         inc=sv)
        tonesc = kb.emit("vector", lambda v: v.memset(ones_col_f[:, :], 1.0),
                         inc=sv)
        teps = kb.emit("vector", lambda v: v.memset(eps_col[:, :], EPS),
                       inc=sv)
        trcc = kb.emit("vector", lambda v: v.memset(rcc[:, :], RC),
                       inc=sv)

        lam_toks = []
        for i, lt in enumerate((lq1_in, lk1_in, lq2_in, lk2_in)):
            lam_toks.append(kb.emit(
                "sync", lambda s, lt=lt, i=i: s.dma_start(
                    lcol[:, i:i + 1], lt.ap()[:]), inc=sl, inc_k=16))
        d1 = kb.emit("tensor", lambda t: t.matmul(
            ps1[0:1, 0:1], lcol[:, 0:1], lcol[:, 1:2], start=True, stop=True),
            waits=[lam_toks[-1]], inc=st)
        d2 = kb.emit("tensor", lambda t: t.matmul(
            ps1[0:1, 1:2], lcol[:, 2:3], lcol[:, 3:4], start=True, stop=True),
            waits=[lam_toks[-1]], inc=st)
        e1 = kb.emit("scalar", lambda s: s.activation(
            scr[:, 0:1], ps1[0:1, 0:1], AF.Exp), waits=[d1], inc=ss)
        e2 = kb.emit("scalar", lambda s: s.activation(
            scr[:, 1:2], ps1[0:1, 1:2], AF.Exp), waits=[d2], inc=ss)
        lm = kb.emit("vector", lambda v: v.scalar_tensor_tensor(
            scr[:, 2:3], scr[:, 0:1], 0.8, scr[:, 1:2], ALU.add,
            ALU.subtract), waits=[e1, e2], inc=sv)
        nl = kb.emit("vector", lambda v: v.tensor_scalar(
            scr[:, 3:4], scr[:, 2:3], -1.0, None, ALU.mult),
            waits=[lm], inc=sv)
        bc = kb.emit("tensor", lambda t: t.matmul(
            ps1[:, 2:3], ones_row_f[:, :], scr[0:1, 3:4], start=True,
            stop=True), waits=[nl, tonesf], inc=st)
        nlmb = kb.emit("vector", lambda v: v.tensor_copy(
            nlam_b[:, :], ps1[:, 2:3]), waits=[bc], inc=sv)

        def prep_a(w_in, rows, cols, b_in, bs_row, widx, pa_war):
            myssl = ssl[0:2] if widx == 0 else ssl[5:7]
            myssa = ssa[0:2] if widx == 0 else ssa[2:4]
            ntile = rows // 128
            per_wave = (8 * D) // cols
            hw = (per_wave + 1) // 2
            nhw = ntile // hw
            red_toks = []
            for wv in range(nhw):
                sl = (wv % 2) * hw * cols
                war = red_toks[-(hw + 1):-(hw + 1) + 1] if wv >= 2 else \
                    pa_war.get("pa%d" % (wv % 2), [])
                dtok = kb.emit(
                    "sync", lambda s, wv=wv, sl=sl: s.dma_start(
                        ap(stage, sl, [[8 * D, 64], [cols, hw], [1, cols]]),
                        ap(w_in, wv * hw * 128 * cols,
                           [[cols, 64], [128 * cols, hw], [1, cols]])),
                    waits=war, inc=myssl[wv % 2], inc_k=16)
                dtok2 = kb.emit(
                    "scalar", lambda s, wv=wv, sl=sl: s.dma_start(
                        ap(stage, sl + 64 * 8 * D,
                           [[8 * D, 64], [cols, hw], [1, cols]]),
                        ap(w_in, wv * hw * 128 * cols + 64 * cols,
                           [[cols, 64], [128 * cols, hw], [1, cols]])),
                    waits=war, inc=myssa[wv % 2], inc_k=16)
                for j in range(hw):
                    r = wv * hw + j
                    red_toks.append(kb.emit(
                        "vector", lambda v, r=r, j=j, sl=sl: v.reduce_sum(
                            acc[:, widx * 40 + r:widx * 40 + r + 1],
                            stage[:, sl + j * cols:sl + (j + 1) * cols],
                            axis=AX, apply_absolute_value=True),
                        waits=[dtok, dtok2], inc=sv))
            tsum = kb.emit("vector", lambda v: v.reduce_sum(
                acc[:, widx * 40 + 33:widx * 40 + 34],
                acc[:, widx * 40:widx * 40 + ntile], axis=AX),
                waits=red_toks, inc=sv)
            tps = kb.emit("tensor", lambda t: t.matmul(
                ps1[0:1, 4 + widx:5 + widx],
                acc[:, widx * 40 + 33:widx * 40 + 34], ones_col_f[:, :],
                start=True, stop=True), waits=[tsum, tonesc], inc=st)
            wc = 4 + widx
            tws = kb.emit("vector", lambda v: v.tensor_scalar(
                scr[:, wc:wc + 1], ps1[0:1, 4 + widx:5 + widx],
                1.0 / (rows * cols), EPS, ALU.mult, ALU.max),
                waits=[tps], inc=sv)
            ic = 6 + widx
            tiw = kb.emit("vector", lambda v: v.reciprocal(
                scr[:, ic:ic + 1], scr[:, wc:wc + 1]), waits=[tws], inc=sv)
            tb1 = kb.emit("tensor", lambda t: t.matmul(
                ps1[:, 6 + widx:7 + widx], ones_row_f[:, :],
                scr[0:1, ic:ic + 1], start=True, stop=True),
                waits=[tiw, tonesf], inc=st)
            tiwb = kb.emit("vector", lambda v: v.tensor_copy(
                iws_b[:, widx:widx + 1], ps1[:, 6 + widx:7 + widx]),
                waits=[tb1], inc=sv)
            tq = kb.emit("vector", lambda v: v.tensor_scalar(
                scr[:, 8 + widx:9 + widx], scr[:, wc:wc + 1], 1.0 / QB, None,
                ALU.mult), waits=[tws], inc=sv)
            tb2 = kb.emit("tensor", lambda t: t.matmul(
                ps1[:, 8 + widx:9 + widx], ones_row_f[:, :],
                scr[0:1, 8 + widx:9 + widx], start=True, stop=True),
                waits=[tq, tonesf], inc=st)
            tdqb = kb.emit("vector", lambda v: v.tensor_copy(
                dq_b[:, widx:widx + 1], ps1[:, 8 + widx:9 + widx]),
                waits=[tb2], inc=sv)
            nb = bs_row.shape[1]
            bo_ = widx * HF
            tb3 = kb.emit("sync", lambda s: s.dma_start(
                brow[:, bo_:bo_ + nb], b_in.ap()[:]),
                inc=(sb1 if widx == 0 else sb2), inc_k=16)
            tq2 = kb.emit("vector", lambda v: v.tensor_scalar(
                scr[:, 10 + widx:11 + widx], scr[:, ic:ic + 1], QB, None,
                ALU.mult), waits=[tiw], inc=sv)
            tbs = kb.emit("vector", lambda v: v.tensor_scalar(
                bs_row[:, :], brow[:, bo_:bo_ + nb],
                scr[0:1, 10 + widx:11 + widx],
                None, ALU.mult), waits=[tb3, tq2], inc=sv)
            return {"w_in": w_in, "rows": rows, "cols": cols, "widx": widx,
                    "red": red_toks, "tiwb": tiwb,
                    "pa0": [red_toks[(nhw - 1) * hw - 1]],
                    "pa1": [red_toks[nhw * hw - 1]]}

        def prep_b(sta, dst_ap, stage_last):
            w_in, rows, cols = sta["w_in"], sta["rows"], sta["cols"]
            widx, red_toks, tiwb = sta["widx"], sta["red"], sta["tiwb"]
            qeng = "sync" if widx == 0 else "scalar"
            qsem = ssl[2] if widx == 0 else ssl[7]
            ntile = rows // 128
            per_wave = (8 * D) // cols
            t1s, t3s, mts_prev, evs = {}, {}, {}, {}
            nch = cols // 128
            grp = max(1, per_wave // 2)
            tld_g = {}

            def emit_t1(r):
                j = r % per_wave
                tp = (r % 2) * HF
                if r % grp == 0:
                    war = [t1s[r - per_wave + grp - 1]] \
                        if r >= per_wave else \
                        red_toks[-1:] + stage_last.get(
                            "t1h%d" % (j // max(1, per_wave // 2)), [])
                    prev = tld_g.get(r // grp - 1)
                    tld_g[r // grp] = kb.emit(
                        qeng, lambda s, r=r, j=j: s.dma_start(
                            ap(stage, j * cols,
                               [[8 * D, 128], [cols, grp], [1, cols]]),
                            ap(w_in, r * 128 * cols,
                               [[cols, 128], [128 * cols, grp], [1, cols]])),
                        waits=war + ([prev] if prev else []),
                        inc=qsem, inc_k=16)
                tld = [tld_g[r // grp]]
                t1s[r] = kb.emit(
                    "scalar", lambda s_, j=j, tp=tp: s_.activation(
                        ttmp[:, tp:tp + cols],
                        stage[:, j * cols:(j + 1) * cols],
                        AF.Identity, bias=rcc[:, :],
                        scale=iws_b[:, widx:widx + 1]),
                    waits=tld + [tiwb, trcc] +
                    ([t3s[r - 2]] if r >= 2 else []),
                    inc=ss)

            emit_t1(0)
            for r in range(ntile):
                j = r % per_wave
                tp = (r % 2) * HF
                t2 = kb.emit("vector", lambda v, tp=tp: v.tensor_scalar(
                    ttmp[:, tp:tp + cols], ttmp[:, tp:tp + cols],
                    RC, 1.0,
                    ALU.subtract, ALU.min), waits=[t1s[r]], inc=sv)
                t3 = kb.emit("vector", lambda v, tp=tp: v.tensor_scalar(
                    tbf[:, tp:tp + cols], ttmp[:, tp:tp + cols],
                    -1.0, None, ALU.max),
                    waits=[t2] + list(mts_prev.get(
                        r - 2, stage_last.get("m%d" % (r % 2), []))),
                    inc=sv)
                t3s[r] = t3
                nbat = (nch + 7) // 8
                for bi in range(nbat):
                    c0, c1 = bi * 8, min(nch, (bi + 1) * 8)
                    g = r * nbat + bi
                    pso = (g % 2) * 1024
                    if g >= 2:
                        pw = evs.get(((g - 2) // nbat, (g - 2) % nbat))
                    else:
                        pw = stage_last.get("e%d" % (g % 2), [None])[0]
                    mts = []
                    for c in range(c0, c1):
                        mts.append(kb.emit(
                            "tensor",
                            lambda t, c=c, bi=bi, tp=tp, pso=pso: t.matmul(
                                ps[:, pso + (c % 8) * 128:
                                   pso + (c % 8) * 128 + 128],
                                tbf[:, tp + c * 128:tp + (c + 1) * 128],
                                ident[:, :], start=True, stop=True),
                            waits=[t3, tid] + ([pw] if pw else []),
                            inc=(st if c == c1 - 1 else None)))
                    mts = mts[-1:]
                    if bi == 0 and r + 1 < ntile:
                        emit_t1(r + 1)
                    ev = kb.emit("scalar",
                                 lambda s_, r=r, c0=c0, c1=c1, pso=pso:
                                 s_.activation(
                                     dst_ap(r, c0, c1),
                                     ps[:, pso:pso + (c1 - c0) * 128],
                                     AF.Copy),
                                 waits=mts, inc=ss)
                    evs[(r, bi)] = ev
                mts_prev[r] = mts
            G = ntile * nbat
            new_last = {"t1h0": [t1s[ntile - 1 - per_wave // 2]],
                        "t1h1": [t1s[ntile - 1]],
                        "m0": mts_prev[ntile - 2], "m1": mts_prev[ntile - 1],
                        "e%d" % ((G - 2) % 2):
                            [evs[((G - 2) // nbat, (G - 2) % nbat)]],
                        "e%d" % ((G - 1) % 2):
                            [evs[((G - 1) // nbat, (G - 1) % nbat)]]}
            return new_last

        def w1_dst(r, c0, c1):
            return ap(w1res, c0 * HF + r * 128,
                      [[8 * HF, 128], [HF, c1 - c0], [1, 128]])

        def w2_dst(r, c0, c1):
            return ap(w2res, c0 * D + r * 128,
                      [[NHC * D, 128], [D, c1 - c0], [1, 128]])

        a2 = prep_a(w2_in, D, HF, b2_in, b2s_row, 1, {})
        a1 = prep_a(w1_in, HF, D, b1_in, b1s_row, 0,
                    {"pa0": a2["pa0"], "pa1": a2["pa1"]})
        nl1 = prep_b(a1, w1_dst, {})
        prep_b(a2, w2_dst, nl1)
        kb.flush(blk)
    kb.barrier_reached()

    with ExitStack() as _stk:
        xb = _stk.enter_context(nc.sbuf_tensor("p1_x", [128, NTB * D], F32))
        xn = _stk.enter_context(nc.sbuf_tensor("p1_xn", [128, D], F32))
        xnbf = _stk.enter_context(nc.sbuf_tensor("p1_xnbf", [128, NTB * D], BF16))
        xqbf = _stk.enter_context(nc.sbuf_tensor("p1_xq", [128, NTB * D], BF16))
        colbf = _stk.enter_context(nc.sbuf_tensor("p1_col", [128, NTB], BF16))
        colbf2 = _stk.enter_context(nc.sbuf_tensor("p1_col2", [128, 1], BF16))
        xqT = _stk.enter_context(nc.sbuf_tensor("p1_xqT", [128, 2 * 8 * P1S], BF16))
        xnT = _stk.enter_context(nc.sbuf_tensor("p1_xnT", [128, 8 * 128], BF16))
        hact = _stk.enter_context(nc.sbuf_tensor("p1_h", [128, NTB * HF], BF16))
        rtmp = _stk.enter_context(nc.sbuf_tensor("p1_rt", [128, 1024], F32))
        h2tb = _stk.enter_context(nc.sbuf_tensor("p1_h2", [128, HF], BF16))
        h2T = _stk.enter_context(nc.sbuf_tensor("p1_h2T", [128, 2 * HF], BF16))
        pout = _stk.enter_context(nc.sbuf_tensor("p1_po", [128, 2 * D], F32))
        stt = _stk.enter_context(nc.sbuf_tensor("p1_st", [128, 28], F32))
        cc = _stk.enter_context(nc.sbuf_tensor("p1_cc", [128, 2 * 2 * NTB], F32))
        rrow = _stk.enter_context(nc.sbuf_tensor("p1_rr", [1, (2 + 1) * NTB * 128], BF16))
        psf = _stk.enter_context(nc.psum_tensor("p1_ps", [128, 4096], F32))
        blk = _stk.enter_context(nc.Block())
        R = Regs()
        x_tok = [None, None]
        lnq = {}
        wr1_prev = [None]
        part_wr = {}
        last_rks = [[]]

        def bank(i):
            return psf[:, i * 512:(i + 1) * 512]

        def p1_ln_vec(s, tbs=None):
            par = s % 2
            xb_users = []
            for tb in (tbs if tbs is not None else range(NTB)):
                xs = x_tok[par]
                xv = xb[:, tb * D:(tb + 1) * D]
                b1t = kb.emit("vector", lambda v, xv=xv: v.bn_stats(
                    stt[:, 0:6], xv[:, 0:512]), waits=[xs], inc=sv)
                b2t = kb.emit("vector", lambda v, xv=xv: v.bn_stats(
                    stt[:, 6:12], xv[:, 512:1024]), waits=[xs], inc=sv)
                agt = kb.emit("vector", lambda v: v.bn_aggr(
                    stt[:, 12:14], stt[:, 0:12]), waits=[b1t, b2t], inc=sv)
                lnt = kb.emit("scalar", lambda s_: s_.activation(
                    stt[:, 14:15], stt[:, 13:14], AF.Ln, bias=eps_col[:, :]),
                    waits=[agt, teps], inc=ss)
                rst = kb.emit("scalar", lambda s_: s_.activation(
                    stt[:, 15:16], stt[:, 14:15], AF.Exp, scale=-0.5),
                    waits=[lnt], inc=ss)
                nmt = kb.emit("vector", lambda v: v.tensor_scalar(
                    stt[:, 16:17], stt[:, 12:13], stt[:, 15:16], -1.0,
                    ALU.mult, ALU.mult), waits=[rst], inc=sv)
                xnt = kb.emit("scalar", lambda s_, xv=xv: s_.activation(
                    xn[:, :], xv, AF.Identity, bias=stt[:, 16:17],
                    scale=stt[:, 15:16]), waits=[nmt], inc=ss)
                bft = kb.emit("vector", lambda v, tb=tb: v.tensor_copy(
                    xnbf[:, tb * D:(tb + 1) * D], xn[:, :]),
                    waits=[xnt] + R.waits("xnbf"), inc=sv)
                mx = kb.emit("vector", lambda v: v.reduce_max(
                    stt[:, 0:1], xn[:, :], axis=AX,
                    apply_absolute_value=True), waits=[xnt], inc=sv)
                cl = kb.emit("vector", lambda v: v.tensor_scalar(
                    stt[:, 1:2], stt[:, 0:1], EPS, None, ALU.max),
                    waits=[mx], inc=sv)
                rc = kb.emit("vector", lambda v: v.reciprocal(
                    stt[:, 2:3], stt[:, 1:2]), waits=[cl], inc=sv)
                sq = kb.emit("vector", lambda v: v.tensor_scalar(
                    stt[:, 3:4], stt[:, 2:3], QB, None, ALU.mult),
                    waits=[rc], inc=sv)
                dq = kb.emit("vector", lambda v, tb=tb, par=par:
                             v.tensor_tensor(
                                 cc[:, par * 2 * NTB + tb:
                                    par * 2 * NTB + tb + 1],
                                 stt[:, 1:2], dq_b[:, 0:1], ALU.mult),
                             waits=[cl], inc=sv)
                cb = kb.emit("vector", lambda v, tb=tb: v.tensor_copy(
                    colbf[:, tb:tb + 1], stt[:, 2:3]),
                    waits=[rc] + R.waits("colbf"), inc=sv)
                q1 = kb.emit("vector", lambda v: v.tensor_scalar(
                    xn[:, :], xn[:, :], stt[:, 3:4], RC, ALU.mult, ALU.add),
                    waits=[sq, bft], inc=sv)
                q2 = kb.emit("vector", lambda v, tb=tb: v.tensor_scalar(
                    xqbf[:, tb * D:(tb + 1) * D], xn[:, :], RC, None,
                    ALU.subtract), waits=[q1] + R.waits("xqbf"), inc=sv)
                lnq[("v", s, tb)] = (bft, cb, q2, dq)
                xb_users += [b1t, b2t, xnt]
            R.use("xb", R.waits("xb") + xb_users
                  if tbs and tbs[0] else xb_users)

        def p1_ln_tensor(s):
            par = s % 2
            xn_users, xq_users, cb_users = [], [], []
            xnT_guard = []
            for tb in range(NTB):
                bft, cb, q2, dq = lnq[("v", s, tb)]
                bx = (tb % 2) * 4
                rw7 = R.waits("rb%d" % (tb % 2))
                trr = kb.emit("tensor", lambda t, tb=tb, bx=bx: t.matmul(
                    psf[0:1, bx * 512 + 384:bx * 512 + 512],
                    colbf[:, tb:tb + 1],
                    ident[:, :], start=True, stop=True),
                    waits=[cb, tid] + rw7 + R.waits("b%d" % bx), inc=st)
                rslot = par * NTB + tb
                rr = kb.emit("vector", lambda v, rslot=rslot, bx=bx:
                             v.tensor_copy(
                    rrow[:, rslot * 128:(rslot + 1) * 128],
                    psf[0:1, bx * 512 + 384:bx * 512 + 512]),
                    waits=[trr] + R.waits("rrow%d" % rslot), inc=sv)
                R.use("rb%d" % (tb % 2), [rr])
                cb_users.append(trr)
                rw = R.waits("b%d" % bx) + R.waits("b%d" % (bx + 1)) + [rr]
                mts = []
                for c in range(8):
                    mts.append(kb.emit(
                        "tensor", lambda t, c=c, tb=tb, bx=bx: t.matmul(
                            psf[:, bx * 512 + c * 128:
                                bx * 512 + (c + 1) * 128],
                            xnbf[:, tb * D + c * 128:tb * D + (c + 1) * 128],
                            ident[:, :], start=True, stop=True),
                        waits=[bft, tid] + rw,
                        inc=(st if c == 7 else None)))
                mts = mts[-1:]
                xn_users += mts
                odw = ([wr1_prev[0]] if wr1_prev[0] else []) + xnT_guard
                ev_eng = "scalar" if tb % 2 == 0 else "vector"
                ev_sem = ss if tb % 2 == 0 else sv
                if tb % 2 == 0:
                    ev1 = kb.emit("scalar", lambda s_, bx=bx: s_.activation(
                        xnT[:, :], psf[:, bx * 512:bx * 512 + 1024], AF.Copy),
                        waits=mts + odw, inc=ss)
                else:
                    ev1 = kb.emit("vector", lambda v, bx=bx: v.tensor_copy(
                        xnT[:, :], psf[:, bx * 512:bx * 512 + 1024]),
                        waits=mts + odw, inc=sv)
                xnT_guard = [ev1]
                R.use("b%d" % bx, [ev1]); R.use("b%d" % (bx + 1), [ev1])
                wr1 = kb.emit("scalar", lambda s_, s=s, tb=tb: s_.dma_start(
                    ap(xn_sc, s * P1S + tb * 128,
                       [[S, 128], [128 * S, 8], [1, 128]]),
                    ap(xnT, 0, [[8 * 128, 128], [128, 8], [1, 128]])),
                    waits=[ev1], inc=od_xn, inc_k=16)
                wr1_prev[0] = wr1
                rw = R.waits("b%d" % (bx + 2)) + R.waits("b%d" % (bx + 3))
                mts2 = []
                for c in range(8):
                    mts2.append(kb.emit(
                        "tensor", lambda t, c=c, tb=tb, bx=bx: t.matmul(
                            psf[:, (bx + 2) * 512 + c * 128:
                                (bx + 2) * 512 + (c + 1) * 128],
                            xqbf[:, tb * D + c * 128:tb * D + (c + 1) * 128],
                            ident[:, :], start=True, stop=True),
                        waits=[q2, tid] + rw,
                        inc=(st if c == 7 else None)))
                mts2 = mts2[-1:]
                xq_users += mts2
                dst = ap(xqT, par * 8 * P1S + tb * 1024,
                         [[2 * 8 * P1S, 128], [1, 1024]])
                if tb % 2 == 0:
                    ev2 = kb.emit("vector", lambda v, dst=dst, bx=bx:
                                  v.tensor_copy(dst, ap(
                                      psf, (bx + 2) * 512,
                                      [[4096, 128], [1, 1024]])),
                                  waits=mts2, inc=sv)
                else:
                    ev2 = kb.emit("scalar", lambda s_, dst=dst, bx=bx:
                                  s_.activation(dst, ap(
                                      psf, (bx + 2) * 512,
                                      [[4096, 128], [1, 1024]]), AF.Copy),
                                  waits=mts2, inc=ss)
                R.use("b%d" % (bx + 2), [ev2]); R.use("b%d" % (bx + 3), [ev2])
                lnq[(s, tb)] = (ev2, dq, rr, wr1)
            R.use("xnbf", xn_users)
            R.use("xqbf", xq_users)
            R.use("colbf", cb_users)

        def p1_ffn1(s):
            par = s % 2
            gelu = {}
            mm_last = {}
            for hc in range(8):
                for tb in range(NTB):
                    bi = (hc % 2) * 4 + tb
                    ev2, dq, rr, _ = lnq[(s, tb)]
                    rw = R.waits("b%d" % bi)
                    rk = kb.emit(
                        "tensor", lambda t, bi=bi, tb=tb, par=par, hc=hc:
                        t.matmul(bank(bi),
                                 rrow[:, (par * NTB + tb) * 128:
                                      (par * NTB + tb + 1) * 128],
                                 b1s_row[:, hc * 512:(hc + 1) * 512],
                                 start=True, stop=False),
                        waits=[rr] + rw, inc=st)
                    R.use("rrow%d" % (par * NTB + tb),
                          R.waits("rrow%d" % (par * NTB + tb)) + [rk])
                    last = rk
                    for di in range(8):
                        last = kb.emit(
                            "tensor",
                            lambda t, bi=bi, di=di, hc=hc, par=par, tb=tb:
                            t.matmul(bank(bi),
                                     xqT[:, par * 8 * P1S + tb * 1024 +
                                         di * 128:par * 8 * P1S +
                                         tb * 1024 + di * 128 + 128],
                                     w1res[:, di * HF + hc * 512:
                                           di * HF + (hc + 1) * 512],
                                     start=False, stop=(di == 7)),
                            waits=[ev2], inc=(st if di == 7 else None))
                    mm_last[hc] = last
                    g = kb.emit("scalar", lambda s_, bi=bi, tb=tb, hc=hc:
                                s_.activation(
                                    hact[:, tb * HF + hc * 512:
                                         tb * HF + (hc + 1) * 512],
                                    bank(bi), AF.Gelu,
                                    scale=cc[:, (par * 2 * NTB) + tb:
                                             (par * 2 * NTB) + tb + 1]),
                                waits=[last, dq], inc=ss)
                    gelu[(hc, tb)] = g
                    R.use("b%d" % bi, [g])
                    am = kb.emit("vector", lambda v, tb=tb, hc=hc:
                                 v.reduce_max(
                                     stt[:, 26:27],
                                     hact[:, tb * HF + hc * 512:
                                          tb * HF + (hc + 1) * 512],
                                     axis=AX, apply_absolute_value=True),
                                 waits=[g] + R.waits("amx"), inc=sv)
                    am2 = kb.emit("vector", lambda v, tb=tb, hc=hc:
                                  (v.tensor_copy(stt[:, 22 + tb:23 + tb],
                                                 stt[:, 26:27]) if hc == 0
                                   else
                                   v.tensor_tensor(stt[:, 22 + tb:23 + tb],
                                                   stt[:, 22 + tb:23 + tb],
                                                   stt[:, 26:27], ALU.max)),
                                  waits=[am], inc=sv)
                    R.use("amx", [am2])
            amx = {tb: (sv, sv.n) for tb in range(NTB)}
            return gelu, amx

        def p1_quant_h(s, gelu, amx, tbs, h2ev, trr_bank):
            for tb in tbs:
                cl = kb.emit("vector", lambda v, tb=tb: v.tensor_scalar(
                    stt[:, 18:19], stt[:, 22 + tb:23 + tb], EPS, None,
                    ALU.max), waits=[amx[tb]], inc=sv)
                rc = kb.emit("vector", lambda v: v.reciprocal(
                    stt[:, 19:20], stt[:, 18:19]), waits=[cl], inc=sv)
                sq = kb.emit("vector", lambda v: v.tensor_scalar(
                    stt[:, 20:21], stt[:, 19:20], QB, None, ALU.mult),
                    waits=[rc], inc=sv)
                dq = kb.emit("vector", lambda v, tb=tb: v.tensor_tensor(
                    cc[:, (s % 2) * 2 * NTB + NTB + tb:
                       (s % 2) * 2 * NTB + NTB + tb + 1],
                    stt[:, 18:19], dq_b[:, 1:2], ALU.mult), waits=[cl], inc=sv)
                cb = kb.emit("vector", lambda v: v.tensor_copy(
                    colbf2[:, :], stt[:, 19:20]), waits=[rc], inc=sv)
                rw = R.waits("b%d" % trr_bank)
                trr = kb.emit("tensor", lambda t, trr_bank=trr_bank: t.matmul(
                    psf[0:1, trr_bank * 512:trr_bank * 512 + 128],
                    colbf2[:, :], ident[:, :], start=True, stop=True),
                    waits=[cb, tid] + rw, inc=st)
                rr = kb.emit("vector", lambda v, tb=tb, trr_bank=trr_bank:
                             v.tensor_copy(
                    rrow[:, (2 * NTB + tb) * 128:(2 * NTB + tb + 1) * 128],
                    psf[0:1, trr_bank * 512:trr_bank * 512 + 128]),
                    waits=[trr] + R.waits("rq%d" % tb), inc=sv)
                R.use("b%d" % trr_bank, [rr])
                rnds = []
                h2war = R.waits("h2tb")
                for hh in range(4):
                    r1 = kb.emit("vector", lambda v, tb=tb, hh=hh:
                                 v.tensor_scalar(
                                     rtmp[:, :],
                                     hact[:, tb * HF + hh * 1024:
                                          tb * HF + (hh + 1) * 1024],
                                     stt[:, 20:21], RC, ALU.mult, ALU.add),
                                 waits=[sq] + rnds[-1:] +
                                 [gelu[(hc, tb)] for hc in range(8)], inc=sv)
                    r2 = kb.emit("vector", lambda v, hh=hh: v.tensor_scalar(
                        h2tb[:, hh * 1024:(hh + 1) * 1024], rtmp[:, :],
                        RC, None, ALU.subtract),
                        waits=[r1] + h2war, inc=sv)
                    rnds.append(r2)
                for hb in range(4):
                    bi = hb % 2
                    rw = R.waits("b%d" % bi)
                    mts = []
                    for c in range(8):
                        hc = hb * 8 + c
                        mts.append(kb.emit(
                            "tensor", lambda t, hc=hc, c=c, bi=bi: t.matmul(
                                psf[:, bi * 1024 + c * 128:
                                    bi * 1024 + (c + 1) * 128],
                                h2tb[:, hc * 128:(hc + 1) * 128],
                                ident[:, :], start=True, stop=True),
                            waits=rnds[hb:hb + 1] + [tid] + rw,
                            inc=(st if c == 7 else None)))
                    mts = mts[-1:]
                    dst = ap(h2T, (tb % 2) * HF + hb * 1024,
                             [[2 * HF, 128], [1, 1024]])
                    ev = kb.emit("scalar", lambda s_, dst=dst, bi=bi:
                                 s_.activation(dst, ap(
                                     psf, bi * 1024,
                                     [[4096, 128], [1, 1024]]), AF.Copy),
                                 waits=mts + R.waits("h2s%d" % (tb % 2)),
                                 inc=ss)
                    R.use("b%d" % bi, [ev])
                    h2ev[(tb, hb)] = ev
                R.use("h2tb", [h2ev[(tb, hb)] for hb in range(4)])
                R.use("amx", R.waits("amx") + [cl])
                h2ev[("dq", tb)] = dq
                h2ev[("rr", tb)] = rr
            return h2ev

        def p1_ffn2_pass(s, h2ev, pass_, w2seq):
            par = s % 2
            base_x = par * NTB * D
            tbs = [2 * pass_, 2 * pass_ + 1]
            bank0 = 4 * (1 - pass_)

            def bi_of(tb, dh):
                return bank0 + (tb % 2) * 2 + dh

            rks = {}
            for tb in tbs:
                for dh in range(2):
                    bi = bi_of(tb, dh)
                    rw = R.waits("b%d" % bi)
                    rks[(tb, dh)] = kb.emit(
                        "tensor", lambda t, bi=bi, tb=tb, dh=dh: t.matmul(
                            bank(bi), rrow[:, (2 * NTB + tb) * 128:
                                           (2 * NTB + tb + 1) * 128],
                            b2s_row[:, dh * 512:(dh + 1) * 512],
                            start=True, stop=False,
                            skip_group_check=True),
                        waits=[h2ev[("rr", tb)]] + rw, inc=st)
                R.use("rq%d" % tb, [rks[(tb, 0)], rks[(tb, 1)]])
            mm_last = {}
            for ck in range(NHC // 4):
                last = None
                for hq in range(4):
                    hc = ck * 4 + hq
                    hb = hc // 8
                    for tb in tbs:
                        for dh in range(2):
                            bi = bi_of(tb, dh)
                            last = kb.emit(
                                "tensor",
                                lambda t, bi=bi, hc=hc, tb=tb, dh=dh:
                                t.matmul(bank(bi),
                                         h2T[:, (tb % 2) * HF + hc * 128:
                                             (tb % 2) * HF + hc * 128 + 128],
                                         w2res[:, hc * D + dh * 512:
                                               hc * D + (dh + 1) * 512],
                                         start=False, stop=(hc == NHC - 1),
                                         skip_group_check=True),
                                waits=[h2ev[(tb, hb)],
                                       rks[(tb, dh)]],
                                inc=(st if (dh == 1 and tb == tbs[-1])
                                     else None))
                    mm_last[hc] = last
            for tb in tbs:
                for dh in range(2):
                    R.use("b%d" % bi_of(tb, dh), [mm_last[NHC - 1]])
                R.use("h2s%d" % (tb % 2), [mm_last[NHC - 1]])
            return (tbs, bi_of, mm_last[NHC - 1], h2ev)

        def p1_ffn2_evac(s, state):
            par = s % 2
            tbs, bi_of, mm_tok, h2ev = state
            evs_all = []
            for tb in tbs:
                po = tb % 2
                pw = [part_wr[po]] if po in part_wr else []
                for dh in range(2):
                    bi = bi_of(tb, dh)
                    ev = kb.emit(
                        "vector",
                        lambda v, bi=bi, tb=tb, dh=dh, po=po:
                        v.tensor_scalar(
                            pout[:, po * D + dh * 512:
                                 po * D + (dh + 1) * 512],
                            bank(bi),
                            cc[:, par * 2 * NTB + NTB + tb:
                               par * 2 * NTB + NTB + tb + 1],
                            None, ALU.mult),
                        waits=[mm_tok, h2ev[("dq", tb)]] + pw, inc=sv)
                    R.use("b%d" % bi, [ev])
                    evs_all.append(ev)
                part_wr[po] = kb.emit(
                    "scalar", lambda s_, s=s, tb=tb, po=po: s_.dma_start(
                        part_sc.ap()[s * P1S + tb * 128:
                                     s * P1S + (tb + 1) * 128, :],
                        pout[:, po * D:(po + 1) * D]),
                    waits=evs_all[-2:], inc=od_part[po], inc_k=16)
            return evs_all

        prev_evs = {0: [], 1: []}

        def emit_xload(s):
            par = s % 2
            x_tok[par] = kb.emit("sync", lambda s_, s=s: s_.dma_start(
                ap(xb, 0, [[NTB * D, 128], [D, NTB], [1, D]]),
                ap(x_in, s * P1S * D, [[D, 128], [128 * D, NTB], [1, D]])),
                waits=R.waits("xb"), inc=sx[par], inc_k=16)

        emit_xload(0)
        p1_ln_vec(0, [0, 1])
        p1_ln_vec(0, [2, 3])
        p1_ln_tensor(0)
        w2seq = {"mm": {}, "n": 0}
        for s in range(N1):
            par = s % 2
            if s + 1 < N1:
                emit_xload(s + 1)
            gelu, amx = p1_ffn1(s)
            h2ev = {}
            p1_quant_h(s, gelu, amx, [0, 1], h2ev, 7)
            st_a = p1_ffn2_pass(s, h2ev, 0, w2seq)
            if s + 1 < N1:
                p1_ln_vec(s + 1, [0, 1])
            p1_quant_h(s, gelu, amx, [2, 3], h2ev, 3)
            ev_a = p1_ffn2_evac(s, st_a)
            st_b = p1_ffn2_pass(s, h2ev, 1, w2seq)
            if s + 1 < N1:
                p1_ln_vec(s + 1, [2, 3])
            ev_b = p1_ffn2_evac(s, st_b)
            if s + 1 < N1:
                p1_ln_tensor(s + 1)
            prev_evs[par] = ev_a + ev_b
        kb.emit("sync", lambda s: s.nop(),
                waits=[(od_xn, od_xn.n)] +
                [(p, p.n) for p in od_part])
        kb.flush(blk)
    kb.barrier_reached()
    _stk01.close()

    k_T = nc.alloc_sbuf_tensor("k_T", [128, NH * S], BF16)
    v_eff = nc.alloc_sbuf_tensor("v_eff", [128, KC * 4 * 129], BF16)
    with ExitStack() as _stk:
        xnT2 = _stk.enter_context(nc.sbuf_tensor("p2_xnT", [128, 4 * S], BF16))
        wbuf = _stk.enter_context(nc.sbuf_tensor("p2_w", [128, 8 * D], BF16))
        wbuf2 = _stk.enter_context(nc.sbuf_tensor("p2_w2", [128, 8 * D], BF16))
        wvb = _stk.enter_context(nc.sbuf_tensor("p2_wv", [128, 8 * 512], BF16))
        qstg = _stk.enter_context(nc.sbuf_tensor("p2_qs", [128, 6 * 512], BF16))
        bcol = _stk.enter_context(nc.sbuf_tensor("p2_bc", [128, 16], F32))
        brow2 = _stk.enter_context(nc.sbuf_tensor("p2_br", [1, D], F32))
        bvrow = _stk.enter_context(nc.sbuf_tensor("p2_bvr", [1, 512], BF16))
        ps2 = _stk.enter_context(nc.psum_tensor("p2_ps", [128, 4096], F32))
        blk = _stk.enter_context(nc.Block())
        xn_toks = {}

        def load_xn(tt_, war):
            xn_toks[tt_] = kb.emit(
                "sync", lambda s, tt_=tt_: s.dma_start(
                    ap(xnT2, (tt_ % max(1, S // 1024)) * 512,
                       [[4 * S, 128], [S // 2, 8], [1, 512]]),
                    ap(xn_sc, tt_ * 512,
                       [[S, 128], [128 * S, 8], [1, 512]])),
                waits=[(od_xn, od_xn.n)] + war, inc=sxn, inc_k=16)

        for tt_ in range(S // 1024):
            load_xn(tt_, [])
        wq_tok = kb.emit("gpsimd", lambda g: g.dma_start(
            ap(wbuf2, 0, [[8 * D, 128], [D, 8], [1, D]]),
            ap(wq_in, 0, [[D, 128], [128 * D, 8], [1, D]])),
            inc=gd2, inc_k=16)
        t_b = kb.emit("sync", lambda s: s.dma_start(
            brow2[:, :], bq_in.ap()[:]), inc=sbr, inc_k=16)
        bq_toks = []
        for c in range(8):
            mm = kb.emit("tensor", lambda t, c=c: t.matmul(
                ps2[:, 3584 + c:3584 + c + 1],
                brow2[:, c * 128:(c + 1) * 128], ones_col_f[0:1, 0:1],
                start=True, stop=True), waits=[t_b, tonesc], inc=st)
            bq_toks.append(mm)
        t_bqc = kb.emit("vector", lambda v: v.tensor_copy(
            bcol[:, 0:8], ps2[:, 3584:3592]), waits=bq_toks, inc=sv)
        t_b2 = kb.emit("sync", lambda s: s.dma_start(
            brow2[:, :], bk_in.ap()[:]), waits=bq_toks, inc=sbr, inc_k=16)
        bk_toks = []
        for c in range(8):
            mm = kb.emit("tensor", lambda t, c=c: t.matmul(
                ps2[:, 3584 + c:3584 + c + 1],
                brow2[:, c * 128:(c + 1) * 128], ones_col_f[0:1, 0:1],
                start=True, stop=True), waits=[t_b2, t_bqc], inc=st)
            bk_toks.append(mm)
        t_bkc = kb.emit("vector", lambda v: v.tensor_copy(
            bcol[:, 8:16], ps2[:, 3584:3592]), waits=bk_toks, inc=sv)
        tlv = kb.emit("gpsimd", lambda g: g.dma_start(
            ap(wbuf, 0, [[8 * D, 128], [D, 8], [1, D]]),
            ap(wv_in, 0, [[D, 128], [128 * D, 8], [1, D]])),
            inc=gd, inc_k=16)
        wv_toks = []
        for di in range(8):
            tc = kb.emit("vector", lambda v, di=di: v.scalar_tensor_tensor(
                wvb[:, di * 512:(di + 1) * 512],
                wbuf[:, di * D + 512:(di + 1) * D],
                nlam_b[:, :], wbuf[:, di * D:di * D + 512],
                ALU.mult, ALU.add), waits=[tlv, nlmb], inc=sv)
            wv_toks.append(tc)
        t_bv = kb.emit("sync", lambda s: s.dma_start(
            brow2[:, :], bv_in.ap()[:]), waits=bk_toks, inc=sbr, inc_k=16)
        t_bv2 = kb.emit("vector", lambda v: v.scalar_tensor_tensor(
            bvrow[:, :], brow2[:, 512:1024], nlam_b[0:1, 0:1],
            brow2[:, 0:512], ALU.mult, ALU.add), waits=[t_bv, nlmb], inc=sv)

        NTT = S // 512
        q_wr = []

        def proj(w_in, dst_fn, wload_waits, tts, bank_war=None, wb=None,
                 preload=None):
            if wb is None:
                wb = wbuf
            if preload is None:
                preload = kb.emit("gpsimd", lambda g, wb=wb: g.dma_start(
                    ap(wb, 0, [[8 * D, 128], [D, 8], [1, D]]),
                    ap(w_in, 0, [[D, 128], [128 * D, 8], [1, D]])),
                    waits=wload_waits, inc=gd, inc_k=16)
            wtoks = [preload] * 8
            out_toks = []
            last_mm = []
            nt = len(tts)
            for do in range(8):
                for ti, tt in enumerate(tts):
                    if do:
                        war = [out_toks[(do - 1) * nt + ti]]
                    else:
                        war = [bank_war[ti]] if bank_war else []
                    last = None
                    for di in range(8):
                        w0 = [t_bkc] if (do == 0 and di == 0) else []
                        last = kb.emit(
                            "tensor", lambda t, do=do, di=di, tt=tt, wb=wb:
                            t.matmul(ps2[:, tt * 512:(tt + 1) * 512],
                                     wb[:, di * D + do * 128:
                                        di * D + do * 128 + 128],
                                     xnT2[:, di * (S // 2) +
                                          (tt % max(1, S // 1024)) * 512:
                                          di * (S // 2) +
                                          (tt % max(1, S // 1024)) * 512 +
                                          512],
                                     start=(di == 0), stop=(di == 7)),
                            waits=[wtoks[di], xn_toks[tt]] + war + w0,
                            inc=(st if di == 7 else None))
                    last_mm.append(last)
                    out_toks.append(dst_fn(do, tt, [last]))
            return out_toks, last_mm, preload

        def q_dst(do, tt, waits):
            par = len(q_wr) % 6
            ev = kb.emit("scalar", lambda s_, do=do, tt=tt, par=par:
                         s_.activation(
                             qstg[:, par * 512:(par + 1) * 512],
                             ps2[:, tt * 512:(tt + 1) * 512],
                             AF.Identity, bias=bcol[:, do:do + 1]),
                         waits=waits + [t_bqc] +
                         ([q_wr[-6]] if len(q_wr) >= 6 else []),
                         inc=ss)
            wr = kb.emit("sync", lambda s, do=do, tt=tt, par=par: s.dma_start(
                ap(q_sc, do * 128 * S + tt * 512, [[S, 128], [1, 512]]),
                qstg[:, par * 512:(par + 1) * 512]), waits=[ev],
                inc=od_q[par], inc_k=16)
            q_wr.append(wr)
            return ev

        def k_dst(do, tt, waits):
            return kb.emit("scalar", lambda s_, do=do, tt=tt: s_.activation(
                k_T[:, do * S + tt * 512:do * S + (tt + 1) * 512],
                ps2[:, tt * 512:(tt + 1) * 512],
                AF.Identity, bias=bcol[:, 8 + do:9 + do]),
                waits=waits + [t_bkc], inc=ss)

        q0, q_mm0, wq_t = proj(wq_in, q_dst, [], list(range(NTT // 2)),
                               wb=wbuf2, preload=wq_tok)
        k0, k_mm0, wk_t = proj(wk_in, k_dst, wv_toks,
                               list(range(NTT // 2)),
                               bank_war=q0[-(NTT // 2):])

        t_vo = kb.emit("vector", lambda v: v.memset(
            ap(v_eff, 128, [[KC * 4 * 129, 128], [129, KC * 4], [1, 1]]),
            1.0), inc=sv)
        vev = []

        def v_chunk(kc, guard):
            bi = kc % 4
            po = ps2[:, bi * 512:(bi + 1) * 512]
            war = []
            if kc >= 4:
                war.append(vev[kc - 4])
            if (kc % (KC // 2)) < 4 and bi in guard:
                war.append(guard[bi])
            rk = kb.emit("tensor", lambda t, po=po: t.matmul(
                po, ones_row[:, :], bvrow[:, :], start=True, stop=False),
                waits=[t_bv2, tones] + war, inc=st)
            last = rk
            for di in range(8):
                last = kb.emit("tensor", lambda t, po=po, di=di, kc=kc:
                               t.matmul(po,
                                        xnT2[:, di * (S // 2) +
                                             (kc % (KC // 2)) * 128:
                                             di * (S // 2) +
                                             (kc % (KC // 2)) * 128 + 128],
                                        wvb[:, di * 512:(di + 1) * 512],
                                        start=False, stop=(di == 7)),
                               waits=[wv_toks[di], xn_toks[kc // 4]],
                               inc=(st if di == 7 else None))
            dst = ap(v_eff, kc * 4 * 129,
                     [[KC * 4 * 129, 128], [129, 4], [1, 128]])
            src = ap(ps2, bi * 512, [[4096, 128], [128, 4], [1, 128]])
            vev.append(kb.emit("vector", lambda v, dst=dst, src=src:
                               v.tensor_copy(dst, src),
                               waits=[last, t_vo], inc=sv))

        guard0 = {tt: k0[-(NTT // 2) + ti]
                  for ti, tt in enumerate(range(NTT // 2)) if tt < 4}
        for kc in range(KC // 2):
            v_chunk(kc, guard0)
        for tt_ in range(NTT // 2, NTT):
            load_xn(tt_, [vev[(tt_ - NTT // 2 + 1) * 4 - 1]])
        q1, _, _ = proj(wq_in, q_dst, [], list(range(NTT // 2, NTT)),
                        wb=wbuf2, preload=wq_t)
        k1, _, _ = proj(wk_in, k_dst, [], list(range(NTT // 2, NTT)),
                        bank_war=q1[-(NTT - NTT // 2):], preload=wk_t)
        nt1 = NTT - NTT // 2
        guard1 = {tt: k1[-nt1 + ti]
                  for ti, tt in enumerate(range(NTT // 2, NTT)) if tt < 4}
        for kc in range(KC // 2, KC):
            v_chunk(kc, guard1)
        kb.emit("sync", lambda s: s.nop(),
                waits=[(q, q.n) for q in od_q])
        kb.flush(blk)
    kb.barrier_reached()

    with ExitStack() as _stk:
        wo = _stk.enter_context(nc.sbuf_tensor("p3_wo", [128, 8 * D], BF16))
        qb3 = _stk.enter_context(nc.sbuf_tensor("p3_q", [128, 2 * NH * P3S], BF16))
        expT = _stk.enter_context(nc.sbuf_tensor("p3_e", [128, 2 * KC * P3S], BF16))
        attn = _stk.enter_context(nc.sbuf_tensor("p3_a", [128, 4 * D], BF16))
        zbf = _stk.enter_context(nc.sbuf_tensor("p3_z", [128, D], BF16))
        lbc = _stk.enter_context(nc.sbuf_tensor("p3_lbc", [128, 8], BF16))
        nT = _stk.enter_context(nc.sbuf_tensor("p3_nT", [128, 8 * P3S], BF16))
        partb = _stk.enter_context(nc.sbuf_tensor("p3_pp", [128, 2 * NQB * D], F32))
        xb3 = _stk.enter_context(nc.sbuf_tensor("p3_x", [128, NQB * D], F32))
        outb = _stk.enter_context(nc.sbuf_tensor("p3_o", [128, 2 * D], F32))
        st3 = _stk.enter_context(nc.sbuf_tensor("p3_st", [128, 24], F32))
        lncols = _stk.enter_context(nc.sbuf_tensor("p3_ln", [128, 16], F32))
        borow = _stk.enter_context(nc.sbuf_tensor("p3_bo", [1, D], BF16))
        brow3 = _stk.enter_context(nc.sbuf_tensor("p3_br", [1, 512], F32))
        wstg3 = _stk.enter_context(nc.sbuf_tensor("p3_stg", [128, 512], F32))
        wraw = _stk.enter_context(nc.sbuf_tensor("p3_wraw", [128, 512], BF16))
        pqk = _stk.enter_context(nc.psum_tensor("p3_qk", [128, 3072], F32))
        ppv = _stk.enter_context(nc.psum_tensor("p3_pv", [128, 1024], F32))
        pms = pqk
        blk = _stk.enter_context(nc.Block())
        R3 = Regs()
        def row_to_cols(v_in, col_off, prevs):
            toks = []
            for hf in range(2):
                tl = kb.emit("sync", lambda s, hf=hf: s.dma_start(
                    brow3[:, :], v_in.ap()[hf * 512:(hf + 1) * 512]),
                    waits=prevs + toks[-1:], inc=spro, inc_k=16)
                for c in range(4):
                    toks.append(kb.emit(
                        "tensor", lambda t, c=c, hf=hf, col_off=col_off:
                        t.matmul(ppv[:, col_off + hf * 4 + c:
                                     col_off + hf * 4 + c + 1],
                                 brow3[:, c * 128:(c + 1) * 128],
                                 ones_col_f[0:1, 0:1], start=True, stop=True),
                        waits=[tl, tonesc], inc=st))
            return toks

        lw_toks = row_to_cols(lnw_in, 0, [])
        t_lwc = kb.emit("vector", lambda v: v.tensor_scalar(
            lncols[:, 0:8], ppv[:, 0:8], P02, None, ALU.mult),
            waits=lw_toks, inc=sv)
        lb_toks = row_to_cols(lnb_in, 8, lw_toks + [t_lwc])
        t_lbc = kb.emit("vector", lambda v: v.tensor_scalar(
            lncols[:, 8:16], ppv[:, 8:16], P02, None, ALU.mult),
            waits=lb_toks, inc=sv)
        t_lbb = kb.emit("vector", lambda v: v.tensor_copy(
            lbc[:, 0:8], lncols[:, 8:16]), waits=[t_lbc], inc=sv)
        t_bo2 = None
        for hf in range(2):
            t_bo = kb.emit("sync", lambda s, hf=hf: s.dma_start(
                brow3[:, :], bo_in.ap()[hf * 512:(hf + 1) * 512]),
                waits=lb_toks + ([t_bo2] if t_bo2 else []), inc=spro,
                inc_k=16)
            t_bo2 = kb.emit("vector", lambda v, hf=hf: v.tensor_copy(
                borow[:, hf * 512:(hf + 1) * 512], brow3[:, :]),
                waits=[t_bo], inc=sv)
        bo_mms = {0: [], 1: []}
        for dh in range(2):
            bo_mms[dh].append(kb.emit("tensor", lambda t, dh=dh: t.matmul(
                ppv[0:1, dh * 512:dh * 512 + 512], ones_row[0:1, 0:1],
                borow[:, dh * 512:(dh + 1) * 512], start=True, stop=False,
                skip_group_check=True),
                waits=[t_bo2, tones, t_lbc], inc=st))
        wo_toks8 = {}
        prev_parts = {}
        prev_tw = {}
        for di in range(8):
            for dh in range(2):
                i = di * 2 + dh
                stg_ = wstg3 if i % 2 == 0 else outb
                pw = [prev_parts.get(i - 2, t_bo2), prev_tw.get(i - 2)]
                tl = kb.emit("sync", lambda s, di=di, dh=dh, stg_=stg_:
                             s.dma_start(
                    stg_[:, 0:512],
                    wo_in.ap()[di * 128:(di + 1) * 128,
                               dh * 512:(dh + 1) * 512]),
                    waits=pw, inc=(spro if i % 2 == 0 else od_w),
                    inc_k=16)
                tr = kb.emit("vector", lambda v, stg_=stg_: v.tensor_copy(
                    wraw[:, :], stg_[:, 0:512]),
                    waits=[tl, prev_parts.get(i - 1)], inc=sv)
                tw = kb.emit("vector",
                             lambda v, di=di, dh=dh, stg_=stg_:
                             v.tensor_scalar(
                    wo[:, di * D + dh * 512:di * D + (dh + 1) * 512],
                    stg_[:, 0:512], lncols[:, di:di + 1],
                    None, ALU.mult),
                    waits=[tl, t_lwc], inc=sv)
                mm = kb.emit("tensor", lambda t, di=di, dh=dh: t.matmul(
                    ppv[0:1, dh * 512:dh * 512 + 512], lbc[:, di:di + 1],
                    wraw[:, :], start=False,
                    stop=(di == 7), skip_group_check=True),
                    waits=[tr, t_lbb, bo_mms[dh][-1]], inc=st)
                bo_mms[dh].append(mm)
                prev_parts[i] = mm
                prev_tw[i] = tw
                wo_toks8[(di, dh)] = tw
        wo_toks = [wo_toks8[(di, 1)] for di in range(8)]
        t_bop = kb.emit("vector", lambda v: v.tensor_copy(
            borow[:, :], ppv[0:1, 0:1024]),
            waits=[bo_mms[0][-1], bo_mms[1][-1]], inc=sv)
        R3.use("pv0", [t_bop])
        R3.use("pv1", [t_bop])

        q_tok = [None, None]
        p_tok = [None, None]
        x_tok3 = [None]
        outwr = []
        prev_exp = {0: [], 1: [], 2: []}
        all_evs = {}
        qk_mms = {}

        def emit_loads(s3):
            spar = s3 % 2
            q_tok[spar] = kb.emit("sync", lambda s_, s3=s3, spar=spar:
                                  s_.dma_start(
                ap(qb3, spar * NH * P3S,
                   [[2 * NH * P3S, 128], [P3S, NH], [1, P3S]]),
                ap(q_sc, s3 * P3S, [[S, 128], [128 * S, NH], [1, P3S]])),
                waits=[(q, q.n) for q in od_q] +
                R3.waits("qb%d" % spar), inc=sq3[spar], inc_k=16)
            p_tok[spar] = kb.emit("sync", lambda s_, s3=s3, spar=spar:
                                  s_.dma_start(
                ap(partb, spar * NQB * D,
                   [[2 * NQB * D, 128], [D, NQB], [1, D]]),
                ap(part_sc, s3 * P3S * D, [[D, 128], [128 * D, NQB], [1, D]])),
                waits=[(p, p.n) for p in od_part] + R3.waits("pb%d" % spar),
                inc=sp3[spar], inc_k=16)


        def pv_sub(pu, g):
            ps3, ph, pexp = pu
            gi = ps3 * NH + ph
            epar = gi % 2
            ebase = epar * KC * P3S
            c = ph % 4
            toks = []
            for qbi in range(NQB):
                slot = ((gi * NQB + qbi) % 2) * 512
                rw = R3.waits("pv%d" % (slot // 512)) if g == 0 else []
                for j in range(4):
                    kc = g * 4 + j
                    toks.append(kb.emit(
                        "tensor",
                        lambda t, kc=kc, qbi=qbi, slot=slot, c=c, ebase=ebase:
                        t.matmul(ppv[:, slot:slot + 129],
                                 expT[:, ebase + kc * P3S + qbi * 128:
                                      ebase + kc * P3S + qbi * 128 + 128],
                                 v_eff[:, kc * 4 * 129 + c * 129:
                                       kc * 4 * 129 + (c + 1) * 129],
                                 start=(kc == 0), stop=(kc == KC - 1),
                                 skip_group_check=True),
                        waits=[pexp[g]] + rw,
                        inc=(st if j == 3 else None)))
            R3.use("e%d_%d" % (epar, g), [toks[-1]])

        def pv_finish(pu, last_tok):
            ps3, ph, pexp = pu
            apar = ps3 % 2
            gi = ps3 * NH + ph
            epar = gi % 2
            pv_evs = []
            for qbi in range(NQB):
                slot = ((gi * NQB + qbi) % 2) * 512
                rc = kb.emit("vector", lambda v, slot=slot, qbi=qbi:
                             v.reciprocal(st3[:, qbi:qbi + 1],
                                          ppv[:, slot + 128:slot + 129]),
                             waits=[last_tok], inc=sv)
                aw = R3.waits("at%d_%d" % (qbi, apar)) if ph == 0 else []
                ev = kb.emit("vector",
                             lambda v, slot=slot, ph=ph, qbi=qbi, apar=apar:
                             v.tensor_scalar(
                                 attn[:, apar * 2 * D + qbi * D + ph * 128:
                                      apar * 2 * D + qbi * D +
                                      (ph + 1) * 128],
                                 ppv[:, slot:slot + 128],
                                 st3[:, qbi:qbi + 1], None, ALU.mult),
                             waits=[rc] + aw, inc=sv)
                R3.use("pv%d" % (slot // 512), [ev])
                pv_evs.append(ev)
            for qbi in range(NQB):
                all_evs.setdefault(ps3, {}).setdefault(qbi, []).append(
                    pv_evs[qbi])

        def emit_unit(s3, h, pend):
            spar = s3 % 2
            gi = s3 * NH + h
            epar = gi % 2
            ebase = epar * KC * P3S
            exp_toks = []
            mms = None
            for g in range(KC // 4):
                gpar = g % 3
                mms = []
                for j in range(4):
                    kc = g * 4 + j
                    mms.append(kb.emit(
                        "tensor",
                        lambda t, h=h, kc=kc, gpar=gpar, spar=spar:
                        t.matmul(pqk[:, gpar * 1024 + (kc % 4) * P3S:
                                     gpar * 1024 + (kc % 4 + 1) * P3S],
                                 k_T[:, h * S + kc * 128:
                                     h * S + (kc + 1) * 128],
                                 qb3[:, spar * NH * P3S + h * P3S:
                                     spar * NH * P3S + (h + 1) * P3S],
                                 start=True, stop=True),
                        waits=[q_tok[spar]] +
                        (exp_toks[g - 3:g - 2] if g >= 3
                         else prev_exp[gpar]),
                        inc=(st if j == 3 else None)))
                mms = mms[-1:]
                ex = kb.emit("scalar",
                             lambda s_, g=g, gpar=gpar, ebase=ebase:
                             s_.activation(
                                 expT[:, ebase + g * 4 * P3S:
                                      ebase + (g + 1) * 4 * P3S],
                                 pqk[:, gpar * 1024:(gpar + 1) * 1024],
                                 AF.Exp, scale=INV_SQRT_HD),
                             waits=mms + R3.waits("e%d_%d" % (epar, g)),
                             inc=ss)
                exp_toks.append(ex)
                if pend is not None:
                    pv_sub(pend, g)
            for g, ex in enumerate(exp_toks):
                prev_exp[g % 3] = [ex]
            qk_mms.setdefault(s3, []).append(mms[-1])
            if pend is not None:
                pv_finish(pend, (st, st.n))
            return exp_toks

        def emit_lnwo(s3):
            spar = s3 % 2
            R3.use("qb%d" % spar, qk_mms[s3])
            apar = s3 % 2
            for qbi in range(NQB):
                av = attn[:, apar * 2 * D + qbi * D:
                          apar * 2 * D + (qbi + 1) * D]
                wa = all_evs[s3][qbi]
                b1t = kb.emit("vector", lambda v, av=av: v.bn_stats(
                    st3[:, 4:10], av[:, 0:512]), waits=wa, inc=sv)
                b2t = kb.emit("vector", lambda v, av=av: v.bn_stats(
                    st3[:, 10:16], av[:, 512:1024]), waits=wa, inc=sv)
                agt = kb.emit("vector", lambda v: v.bn_aggr(
                    st3[:, 2:4], st3[:, 4:16]), waits=[b1t, b2t], inc=sv)
                lnt = kb.emit("scalar", lambda s_: s_.activation(
                    st3[:, 16:17], st3[:, 3:4], AF.Ln, bias=eps_col[:, :]),
                    waits=[agt], inc=ss)
                rst = kb.emit("scalar", lambda s_: s_.activation(
                    st3[:, 17:18], st3[:, 16:17], AF.Exp, scale=-0.5),
                    waits=[lnt], inc=ss)
                nmt = kb.emit("vector", lambda v: v.tensor_scalar(
                    st3[:, 18:19], st3[:, 2:3], st3[:, 17:18], -1.0,
                    ALU.mult, ALU.mult), waits=[rst], inc=sv)
                zt = kb.emit("vector", lambda v, av=av: v.tensor_scalar(
                    zbf[:, :], av, st3[:, 17:18], st3[:, 18:19],
                    ALU.mult, ALU.add),
                    waits=[nmt] + R3.waits("zb"), inc=sv)
                R3.use("at%d_%d" % (qbi, apar), [zt, b1t, b2t])
                rwp = R3.waits("pms") + prev_exp[0]
                mts = []
                for dc in range(8):
                    mts.append(kb.emit("tensor", lambda t, dc=dc: t.matmul(
                        pms[:, dc * 128:(dc + 1) * 128],
                        zbf[:, dc * 128:(dc + 1) * 128], ident[:, :],
                        start=True, stop=True),
                        waits=[zt, tid] + rwp,
                        inc=(st if dc == 7 else None)))
                mts = mts[-1:]
                R3.use("zb", mts)
                dstn = ap(nT, qbi * 128, [[8 * P3S, 128], [P3S, 8], [1, 128]])
                evn = kb.emit("vector", lambda v, dstn=dstn: v.tensor_copy(
                    dstn, ap(pms, 0, [[3072, 128], [128, 8], [1, 128]])),
                    waits=mts + R3.waits("nt%d" % qbi), inc=sv)
                R3.use("pms", [evn])
                prev_exp[0] = prev_exp[0] + [evn]
                wo_mms = []
                fvs = []
                for dh in range(2):
                    slot = ((qbi * 2 + dh) % 2) * 512
                    rw = R3.waits("pv%d" % (slot // 512))
                    rk = kb.emit("tensor", lambda t, slot=slot, dh=dh:
                                 t.matmul(ppv[:, slot:slot + 512],
                                          ones_row[:, :],
                                          borow[:, dh * 512:(dh + 1) * 512],
                                          start=True, stop=False),
                                 waits=[t_bop] + rw, inc=st)
                    last = rk
                    for dc in range(8):
                        last = kb.emit(
                            "tensor",
                            lambda t, slot=slot, dc=dc, dh=dh, qbi=qbi:
                            t.matmul(ppv[:, slot:slot + 512],
                                     nT[:, dc * P3S + qbi * 128:
                                        dc * P3S + qbi * 128 + 128],
                                     wo[:, dc * D + dh * 512:
                                        dc * D + (dh + 1) * 512],
                                     start=False, stop=(dc == 7)),
                            waits=[evn, wo_toks[dc]],
                            inc=(st if dc == 7 else None))
                    wo_mms.append(last)
                    rwo = [(od_y[qbi], od_y[qbi].n)]
                    fv0 = kb.emit("vector",
                                  lambda v, slot=slot, qbi=qbi, dh=dh,
                                  spar=spar:
                                  v.tensor_tensor(
                                      outb[:, qbi * D + dh * 512:
                                           qbi * D + (dh + 1) * 512],
                                      ppv[:, slot:slot + 512],
                                      partb[:, spar * NQB * D + qbi * D +
                                            dh * 512:spar * NQB * D +
                                            qbi * D + (dh + 1) * 512],
                                      ALU.add),
                                  waits=[last, p_tok[spar]] + rwo, inc=sv)
                    R3.use("pv%d" % (slot // 512), [fv0])
                    fv = kb.emit("vector",
                                 lambda v, qbi=qbi, dh=dh:
                                 v.tensor_tensor(
                                     outb[:, qbi * D + dh * 512:
                                          qbi * D + (dh + 1) * 512],
                                     outb[:, qbi * D + dh * 512:
                                          qbi * D + (dh + 1) * 512],
                                     xb3[:, qbi * D + dh * 512:
                                         qbi * D + (dh + 1) * 512],
                                     ALU.add),
                                 waits=[fv0, x_tok3[0]], inc=sv)
                    R3.use("xb3", R3.waits("xb3") + [fv])
                    fvs.append(fv)
                R3.use("nt%d" % qbi, wo_mms)
                R3.use("pb%d" % spar,
                       R3.waits("pb%d" % spar) + fvs if qbi else fvs)
                wr = kb.emit("sync", lambda s_, s3=s3, qbi=qbi: s_.dma_start(
                    y_out.ap()[s3 * P3S + qbi * 128:
                               s3 * P3S + (qbi + 1) * 128, :],
                    outb[:, qbi * D:(qbi + 1) * D]),
                    waits=fvs, inc=od_y[qbi], inc_k=16)
                outwr.append(wr)

        def emit_xload3(s3):
            x_tok3[0] = kb.emit("sync", lambda s_, s3=s3: s_.dma_start(
                ap(xb3, 0, [[NQB * D, 128], [D, NQB], [1, D]]),
                ap(x_in, s3 * P3S * D, [[D, 128], [128 * D, NQB], [1, D]])),
                waits=R3.waits("xb3"), inc=sx3, inc_k=16)

        emit_loads(0)
        pend = None
        lnwo_pend = [None]
        for s3 in range(N3):
            for h in range(NH):
                if h == NH // 2 and s3 + 1 < N3:
                    emit_loads(s3 + 1)
                if h == NH - 2:
                    emit_xload3(s3)
                done = pend
                et = emit_unit(s3, h, pend)
                pend = (s3, h, et)
                if done is not None and done[1] == NH - 1:
                    lnwo_pend[0] = done[0]
                elif lnwo_pend[0] is not None and h == 3:
                    emit_lnwo(lnwo_pend[0])
                    lnwo_pend[0] = None
        pv_sub_tail = pend
        for g in range(KC // 4):
            pv_sub(pv_sub_tail, g)
        pv_finish(pv_sub_tail, (st, st.n))
        if lnwo_pend[0] is not None:
            emit_lnwo(lnwo_pend[0])
        emit_lnwo(N3 - 1)
        kb.flush(blk)
    kb.barrier_reached()

    with nc.Block() as blk:
        kb.emit("sync", lambda s: s.nop(),
                waits=[(q, q.n) for q in od_y])
        kb.flush(blk)
    return nc


def kernel(**inputs):
    x = np.ascontiguousarray(np.asarray(inputs["L0_attn_hidden_states"],
                                        dtype=np.float32))
    B, S, _ = x.shape
    nc = build_nc(S=S)
    names = {
        "wq": "L0_attn_Wq", "bq": "L0_attn_bq", "wk": "L0_attn_Wk",
        "bk": "L0_attn_bk", "wv": "L0_attn_Wv", "bv": "L0_attn_bv",
        "wo": "L0_attn_Wo", "bo": "L0_attn_bo", "lq1": "L0_attn_lambda_q1",
        "lk1": "L0_attn_lambda_k1", "lq2": "L0_attn_lambda_q2",
        "lk2": "L0_attn_lambda_k2", "lnw": "L0_attn_ln_w",
        "lnb": "L0_attn_ln_b", "w1": "L0_ffn_W1", "b1": "L0_ffn_b1",
        "w2": "L0_ffn_W2", "b2": "L0_ffn_b2",
    }
    weights = {k: np.ascontiguousarray(np.asarray(inputs[v], np.float32))
               for k, v in names.items()}
    in_maps = []
    for b in range(B):
        m = {"x": np.ascontiguousarray(x[b])}
        m.update(weights)
        in_maps.append(m)
    res = run_bass_kernel_spmd(nc, in_maps, core_ids=list(range(B)))
    return np.stack([res.results[b]["y"] for b in range(B)], axis=0)


# version 35
# speedup vs baseline: 1.0159x; 1.0159x over previous
import numpy as np
from contextlib import ExitStack

import concourse.bass as bass
import concourse.mybir as mybir
from concourse.bass_utils import run_bass_kernel_spmd

F32 = mybir.dt.float32
BF16 = mybir.dt.bfloat16
F8 = mybir.dt.float8e4
AF = mybir.ActivationFunctionType
ALU = mybir.AluOpType
AX = mybir.AxisListType.X

D = 1024
HF = 4096
NH = 8
HD = 128
RC = 12582912.0
INV_SQRT_HD = 0.08838834764831845
EPS = 1e-5
QB = 127.0
P02 = 0.19999999999999996

ENGINES = ("tensor", "vector", "scalar", "sync", "gpsimd")


class Sem:
    def __init__(self, nc, name, is_dma=False):
        self.h = nc.alloc_semaphore(name)
        self.n = 0
        self.is_dma = is_dma
        self.name = name


class KB:

    def __init__(self, nc):
        self.nc = nc
        self.streams = {e: [] for e in ENGINES}
        self.wm = {e: {} for e in ENGINES}
        self.sems = []

    def sem(self, name, is_dma=False):
        s = Sem(self.nc, name, is_dma)
        self.sems.append(s)
        return s

    def emit(self, engine, fn, waits=(), inc=None, inc_k=1):
        waits = tuple(w for w in waits if w is not None)
        tok = None
        if inc is not None:
            inc.n += inc_k
            tok = (inc, inc.n)
        self.streams[engine].append((fn, waits, inc, inc_k))
        return tok

    def flush(self, block):
        for e in ENGINES:
            ops = self.streams[e]
            if not ops:
                continue

            def runner(eng, ops=ops, e=e):
                wm = self.wm[e]
                for fn, waits, inc, inc_k in ops:
                    for (s, v) in waits:
                        if wm.get(s, 0) < v:
                            eng.wait_ge(s.h, v)
                            wm[s] = v
                    ins = fn(eng)
                    if inc is not None:
                        ins.then_inc(inc.h, inc_k)
            getattr(block, e)(runner)
            self.streams[e] = []

    def barrier_reached(self):
        for e in ENGINES:
            for s in self.sems:
                if not s.is_dma:
                    self.wm[e][s] = s.n


class Regs:

    def __init__(self):
        self.d = {}

    def waits(self, key):
        return list(self.d.get(key, ()))

    def use(self, key, toks):
        self.d[key] = [t for t in toks if t is not None]


def ap(t, off, dims):
    return bass.AP(t, off, [list(d) for d in dims])


def build_nc(S=4096, debug=False):
    nc = bass.Bass("TRN2", target_bir_lowering=False)
    P1S = 512
    P3S = 256
    N1 = S // P1S
    N3 = S // P3S
    KC = S // 128
    NTB = P1S // 128
    NQB = P3S // 128
    NHC = HF // 128

    def din(name, shape):
        return nc.dram_tensor(name, shape, F32, kind="ExternalInput")

    x_in = din("x", [S, D])
    wq_in = din("wq", [D, D]); bq_in = din("bq", [D])
    wk_in = din("wk", [D, D]); bk_in = din("bk", [D])
    wv_in = din("wv", [D, D]); bv_in = din("bv", [D])
    wo_in = din("wo", [D, D]); bo_in = din("bo", [D])
    lq1_in = din("lq1", [HD]); lk1_in = din("lk1", [HD])
    lq2_in = din("lq2", [HD]); lk2_in = din("lk2", [HD])
    lnw_in = din("lnw", [D]); lnb_in = din("lnb", [D])
    w1_in = din("w1", [HF, D]); b1_in = din("b1", [HF])
    w2_in = din("w2", [D, HF]); b2_in = din("b2", [D])

    y_out = nc.dram_tensor("y", [S, D], F32, kind="ExternalOutput")
    sk = "ExternalOutput" if debug else "Internal"
    xn_sc = nc.dram_tensor("xn_sc", [8, 128, S], BF16, kind=sk)
    q_sc = nc.dram_tensor("q_sc", [NH, 128, S], BF16, kind=sk)
    part_sc = nc.dram_tensor("part_sc", [S, D], F32, kind=sk)

    kb = KB(nc)
    gd = kb.sem("gd", True)
    gd2 = kb.sem("gd2", True)
    sl = kb.sem("sl", True)
    sb1 = kb.sem("sb1", True)
    sb2 = kb.sem("sb2", True)
    ssl = [kb.sem("ss%d" % i, True) for i in range(10)]
    od_w2 = kb.sem("od_w2", True)
    sx = [kb.sem("sx%d" % i, True) for i in range(2)]
    sw1 = [kb.sem("sw1%d" % i, True) for i in range(3)]
    sw2 = [kb.sem("sw2%d" % i, True) for i in range(4)]
    sxn = kb.sem("sxn", True)
    sbr = kb.sem("sbr", True)
    sq3 = [kb.sem("sq3%d" % i, True) for i in range(2)]
    sx3 = kb.sem("sx3", True)
    sp3 = [kb.sem("sp3%d" % i, True) for i in range(2)]
    spro = kb.sem("spro", True)
    od_w = kb.sem("od_w", True)
    od_xn = kb.sem("od_xn", True)
    od_part = [kb.sem("od_part%d" % i, True) for i in range(2)]
    od_q = [kb.sem("od_q%d" % i, True) for i in range(6)]
    od_y = [kb.sem("od_y%d" % i, True) for i in range(2)]
    st = kb.sem("st")
    sv = kb.sem("sv")
    ss = kb.sem("ss")
    gp = kb.sem("gp")

    ident = nc.alloc_sbuf_tensor("ident", [128, 128], BF16)
    ones_row = nc.alloc_sbuf_tensor("ones_row", [1, 128], BF16)
    ones_row_f = nc.alloc_sbuf_tensor("ones_row_f", [1, 128], F32)
    ones_col_f = nc.alloc_sbuf_tensor("ones_col_f", [128, 1], F32)
    eps_col = nc.alloc_sbuf_tensor("eps_col", [128, 1], F32)
    nlam_b = nc.alloc_sbuf_tensor("nlam_b", [128, 1], F32)
    iws_b = nc.alloc_sbuf_tensor("iws_b", [128, 2], F32)
    dq_b = nc.alloc_sbuf_tensor("dq_b", [128, 2], F32)
    _stk01 = ExitStack()
    b1s_row = _stk01.enter_context(nc.sbuf_tensor("b1s_row", [1, HF], BF16))
    b2s_row = _stk01.enter_context(nc.sbuf_tensor("b2s_row", [1, D], BF16))
    w1res = _stk01.enter_context(nc.sbuf_tensor("w1res", [128, 8 * HF], F8))
    w2res = _stk01.enter_context(nc.sbuf_tensor("w2res", [128, NHC * D], F8))

    with ExitStack() as _stk:
        stage = _stk.enter_context(nc.sbuf_tensor("p0_stage", [128, 8 * D], F32))
        acc = _stk.enter_context(nc.sbuf_tensor("p0_acc", [128, 80], F32))
        tbf = _stk.enter_context(nc.sbuf_tensor("p0_tbf", [128, 2 * HF], BF16))
        ttmp = _stk.enter_context(nc.sbuf_tensor("p0_tmp", [128, 2 * HF], F32))
        lcol = _stk.enter_context(nc.sbuf_tensor("p0_lcol", [128, 4], F32))
        rcc = _stk.enter_context(nc.sbuf_tensor("p0_rcc", [128, 1], F32))
        scr = _stk.enter_context(nc.sbuf_tensor("p0_sc", [1, 12], F32))
        brow = _stk.enter_context(nc.sbuf_tensor("p0_brow", [1, HF + D], F32))
        ps = _stk.enter_context(nc.psum_tensor("p0_ps", [128, 2048], F32))
        ps1 = _stk.enter_context(nc.psum_tensor("p0_ps1", [128, 512], F32))
        blk = _stk.enter_context(nc.Block())
        tid0 = kb.emit("gpsimd", lambda g: g.memset(ident[:, :], 0.0), inc=gp)
        tid = kb.emit("gpsimd", lambda g: g.affine_select(
            out=ident[:, :], in_=ident[:, :], compare_op=ALU.not_equal,
            fill=1.0, base=0, pattern=[[-1, 128]], channel_multiplier=1),
            waits=[tid0], inc=gp)
        tones = kb.emit("vector", lambda v: v.memset(ones_row[:, :], 1.0),
                        inc=sv)
        tonesf = kb.emit("vector", lambda v: v.memset(ones_row_f[:, :], 1.0),
                         inc=sv)
        tonesc = kb.emit("vector", lambda v: v.memset(ones_col_f[:, :], 1.0),
                         inc=sv)
        teps = kb.emit("vector", lambda v: v.memset(eps_col[:, :], EPS),
                       inc=sv)
        trcc = kb.emit("vector", lambda v: v.memset(rcc[:, :], RC),
                       inc=sv)

        lam_toks = []
        for i, lt in enumerate((lq1_in, lk1_in, lq2_in, lk2_in)):
            lam_toks.append(kb.emit(
                "sync", lambda s, lt=lt, i=i: s.dma_start(
                    lcol[:, i:i + 1], lt.ap()[:]), inc=sl, inc_k=16))
        d1 = kb.emit("tensor", lambda t: t.matmul(
            ps1[0:1, 0:1], lcol[:, 0:1], lcol[:, 1:2], start=True, stop=True),
            waits=[lam_toks[-1]], inc=st)
        d2 = kb.emit("tensor", lambda t: t.matmul(
            ps1[0:1, 1:2], lcol[:, 2:3], lcol[:, 3:4], start=True, stop=True),
            waits=[lam_toks[-1]], inc=st)
        e1 = kb.emit("scalar", lambda s: s.activation(
            scr[:, 0:1], ps1[0:1, 0:1], AF.Exp), waits=[d1], inc=ss)
        e2 = kb.emit("scalar", lambda s: s.activation(
            scr[:, 1:2], ps1[0:1, 1:2], AF.Exp), waits=[d2], inc=ss)
        lm = kb.emit("vector", lambda v: v.scalar_tensor_tensor(
            scr[:, 2:3], scr[:, 0:1], 0.8, scr[:, 1:2], ALU.add,
            ALU.subtract), waits=[e1, e2], inc=sv)
        nl = kb.emit("vector", lambda v: v.tensor_scalar(
            scr[:, 3:4], scr[:, 2:3], -1.0, None, ALU.mult),
            waits=[lm], inc=sv)
        bc = kb.emit("tensor", lambda t: t.matmul(
            ps1[:, 2:3], ones_row_f[:, :], scr[0:1, 3:4], start=True,
            stop=True), waits=[nl, tonesf], inc=st)
        nlmb = kb.emit("vector", lambda v: v.tensor_copy(
            nlam_b[:, :], ps1[:, 2:3]), waits=[bc], inc=sv)

        def prep_a(w_in, rows, cols, b_in, bs_row, widx, pa_war):
            myssl = ssl[0:2] if widx == 0 else ssl[5:7]
            ntile = rows // 128
            per_wave = (8 * D) // cols
            hw = (per_wave + 1) // 2
            nhw = ntile // hw
            red_toks = []
            for wv in range(nhw):
                sl = (wv % 2) * hw * cols
                war = red_toks[-(hw + 1):-(hw + 1) + 1] if wv >= 2 else \
                    pa_war.get("pa%d" % (wv % 2), [])
                dtok = kb.emit(
                    "sync", lambda s, wv=wv, sl=sl: s.dma_start(
                        ap(stage, sl, [[8 * D, 128], [cols, hw], [1, cols]]),
                        ap(w_in, wv * hw * 128 * cols,
                           [[cols, 128], [128 * cols, hw], [1, cols]])),
                    waits=war, inc=myssl[wv % 2], inc_k=16)
                for j in range(hw):
                    r = wv * hw + j
                    red_toks.append(kb.emit(
                        "vector", lambda v, r=r, j=j, sl=sl: v.reduce_sum(
                            acc[:, widx * 40 + r:widx * 40 + r + 1],
                            stage[:, sl + j * cols:sl + (j + 1) * cols],
                            axis=AX, apply_absolute_value=True),
                        waits=[dtok], inc=sv))
            tsum = kb.emit("vector", lambda v: v.reduce_sum(
                acc[:, widx * 40 + 33:widx * 40 + 34],
                acc[:, widx * 40:widx * 40 + ntile], axis=AX),
                waits=red_toks, inc=sv)
            tps = kb.emit("tensor", lambda t: t.matmul(
                ps1[0:1, 4 + widx:5 + widx],
                acc[:, widx * 40 + 33:widx * 40 + 34], ones_col_f[:, :],
                start=True, stop=True), waits=[tsum, tonesc], inc=st)
            wc = 4 + widx
            tws = kb.emit("vector", lambda v: v.tensor_scalar(
                scr[:, wc:wc + 1], ps1[0:1, 4 + widx:5 + widx],
                1.0 / (rows * cols), EPS, ALU.mult, ALU.max),
                waits=[tps], inc=sv)
            ic = 6 + widx
            tiw = kb.emit("vector", lambda v: v.reciprocal(
                scr[:, ic:ic + 1], scr[:, wc:wc + 1]), waits=[tws], inc=sv)
            tb1 = kb.emit("tensor", lambda t: t.matmul(
                ps1[:, 6 + widx:7 + widx], ones_row_f[:, :],
                scr[0:1, ic:ic + 1], start=True, stop=True),
                waits=[tiw, tonesf], inc=st)
            tiwb = kb.emit("vector", lambda v: v.tensor_copy(
                iws_b[:, widx:widx + 1], ps1[:, 6 + widx:7 + widx]),
                waits=[tb1], inc=sv)
            tq = kb.emit("vector", lambda v: v.tensor_scalar(
                scr[:, 8 + widx:9 + widx], scr[:, wc:wc + 1], 1.0 / QB, None,
                ALU.mult), waits=[tws], inc=sv)
            tb2 = kb.emit("tensor", lambda t: t.matmul(
                ps1[:, 8 + widx:9 + widx], ones_row_f[:, :],
                scr[0:1, 8 + widx:9 + widx], start=True, stop=True),
                waits=[tq, tonesf], inc=st)
            tdqb = kb.emit("vector", lambda v: v.tensor_copy(
                dq_b[:, widx:widx + 1], ps1[:, 8 + widx:9 + widx]),
                waits=[tb2], inc=sv)
            nb = bs_row.shape[1]
            bo_ = widx * HF
            tb3 = kb.emit("sync", lambda s: s.dma_start(
                brow[:, bo_:bo_ + nb], b_in.ap()[:]),
                inc=(sb1 if widx == 0 else sb2), inc_k=16)
            tq2 = kb.emit("vector", lambda v: v.tensor_scalar(
                scr[:, 10 + widx:11 + widx], scr[:, ic:ic + 1], QB, None,
                ALU.mult), waits=[tiw], inc=sv)
            tbs = kb.emit("vector", lambda v: v.tensor_scalar(
                bs_row[:, :], brow[:, bo_:bo_ + nb],
                scr[0:1, 10 + widx:11 + widx],
                None, ALU.mult), waits=[tb3, tq2], inc=sv)
            return {"w_in": w_in, "rows": rows, "cols": cols, "widx": widx,
                    "red": red_toks, "tiwb": tiwb,
                    "pa0": [red_toks[(nhw - 1) * hw - 1]],
                    "pa1": [red_toks[nhw * hw - 1]]}

        def prep_b(sta, dst_ap, stage_last):
            w_in, rows, cols = sta["w_in"], sta["rows"], sta["cols"]
            widx, red_toks, tiwb = sta["widx"], sta["red"], sta["tiwb"]
            qeng = "sync" if widx == 0 else "scalar"
            qsem = ssl[2] if widx == 0 else ssl[7]
            ntile = rows // 128
            per_wave = (8 * D) // cols
            t1s, t3s, mts_prev, evs = {}, {}, {}, {}
            nch = cols // 128
            grp = max(1, per_wave // 2)
            tld_g = {}

            def emit_t1(r):
                j = r % per_wave
                tp = (r % 2) * HF
                if r % grp == 0:
                    war = [t1s[r - per_wave + grp - 1]] \
                        if r >= per_wave else \
                        red_toks[-1:] + stage_last.get(
                            "t1h%d" % (j // max(1, per_wave // 2)), [])
                    prev = tld_g.get(r // grp - 1)
                    tld_g[r // grp] = kb.emit(
                        qeng, lambda s, r=r, j=j: s.dma_start(
                            ap(stage, j * cols,
                               [[8 * D, 128], [cols, grp], [1, cols]]),
                            ap(w_in, r * 128 * cols,
                               [[cols, 128], [128 * cols, grp], [1, cols]])),
                        waits=war + ([prev] if prev else []),
                        inc=qsem, inc_k=16)
                tld = [tld_g[r // grp]]
                t1s[r] = kb.emit(
                    "scalar", lambda s_, j=j, tp=tp: s_.activation(
                        ttmp[:, tp:tp + cols],
                        stage[:, j * cols:(j + 1) * cols],
                        AF.Identity, bias=rcc[:, :],
                        scale=iws_b[:, widx:widx + 1]),
                    waits=tld + [tiwb, trcc] +
                    ([t3s[r - 2]] if r >= 2 else []),
                    inc=ss)

            emit_t1(0)
            for r in range(ntile):
                j = r % per_wave
                tp = (r % 2) * HF
                t2 = kb.emit("vector", lambda v, tp=tp: v.tensor_scalar(
                    ttmp[:, tp:tp + cols], ttmp[:, tp:tp + cols],
                    RC, 1.0,
                    ALU.subtract, ALU.min), waits=[t1s[r]], inc=sv)
                t3 = kb.emit("vector", lambda v, tp=tp: v.tensor_scalar(
                    tbf[:, tp:tp + cols], ttmp[:, tp:tp + cols],
                    -1.0, None, ALU.max),
                    waits=[t2] + list(mts_prev.get(
                        r - 2, stage_last.get("m%d" % (r % 2), []))),
                    inc=sv)
                t3s[r] = t3
                nbat = (nch + 7) // 8
                for bi in range(nbat):
                    c0, c1 = bi * 8, min(nch, (bi + 1) * 8)
                    g = r * nbat + bi
                    pso = (g % 2) * 1024
                    if g >= 2:
                        pw = evs.get(((g - 2) // nbat, (g - 2) % nbat))
                    else:
                        pw = stage_last.get("e%d" % (g % 2), [None])[0]
                    mts = []
                    for c in range(c0, c1):
                        mts.append(kb.emit(
                            "tensor",
                            lambda t, c=c, bi=bi, tp=tp, pso=pso: t.matmul(
                                ps[:, pso + (c % 8) * 128:
                                   pso + (c % 8) * 128 + 128],
                                tbf[:, tp + c * 128:tp + (c + 1) * 128],
                                ident[:, :], start=True, stop=True),
                            waits=[t3, tid] + ([pw] if pw else []),
                            inc=(st if c == c1 - 1 else None)))
                    mts = mts[-1:]
                    if bi == 0 and r + 1 < ntile:
                        emit_t1(r + 1)
                    ev = kb.emit("scalar",
                                 lambda s_, r=r, c0=c0, c1=c1, pso=pso:
                                 s_.activation(
                                     dst_ap(r, c0, c1),
                                     ps[:, pso:pso + (c1 - c0) * 128],
                                     AF.Copy),
                                 waits=mts, inc=ss)
                    evs[(r, bi)] = ev
                mts_prev[r] = mts
            G = ntile * nbat
            new_last = {"t1h0": [t1s[ntile - 1 - per_wave // 2]],
                        "t1h1": [t1s[ntile - 1]],
                        "m0": mts_prev[ntile - 2], "m1": mts_prev[ntile - 1],
                        "e%d" % ((G - 2) % 2):
                            [evs[((G - 2) // nbat, (G - 2) % nbat)]],
                        "e%d" % ((G - 1) % 2):
                            [evs[((G - 1) // nbat, (G - 1) % nbat)]]}
            return new_last

        def w1_dst(r, c0, c1):
            return ap(w1res, c0 * HF + r * 128,
                      [[8 * HF, 128], [HF, c1 - c0], [1, 128]])

        def w2_dst(r, c0, c1):
            return ap(w2res, c0 * D + r * 128,
                      [[NHC * D, 128], [D, c1 - c0], [1, 128]])

        a2 = prep_a(w2_in, D, HF, b2_in, b2s_row, 1, {})
        a1 = prep_a(w1_in, HF, D, b1_in, b1s_row, 0,
                    {"pa0": a2["pa0"], "pa1": a2["pa1"]})
        nl1 = prep_b(a1, w1_dst, {})
        prep_b(a2, w2_dst, nl1)
        kb.flush(blk)
    kb.barrier_reached()

    with ExitStack() as _stk:
        xb = _stk.enter_context(nc.sbuf_tensor("p1_x", [128, NTB * D], F32))
        xn = _stk.enter_context(nc.sbuf_tensor("p1_xn", [128, D], F32))
        xnbf = _stk.enter_context(nc.sbuf_tensor("p1_xnbf", [128, NTB * D], BF16))
        xqbf = _stk.enter_context(nc.sbuf_tensor("p1_xq", [128, NTB * D], BF16))
        colbf = _stk.enter_context(nc.sbuf_tensor("p1_col", [128, NTB], BF16))
        colbf2 = _stk.enter_context(nc.sbuf_tensor("p1_col2", [128, 1], BF16))
        xqT = _stk.enter_context(nc.sbuf_tensor("p1_xqT", [128, 2 * 8 * P1S], BF16))
        xnT = _stk.enter_context(nc.sbuf_tensor("p1_xnT", [128, 8 * 128], BF16))
        hact = _stk.enter_context(nc.sbuf_tensor("p1_h", [128, NTB * HF], BF16))
        rtmp = _stk.enter_context(nc.sbuf_tensor("p1_rt", [128, 1024], F32))
        h2tb = _stk.enter_context(nc.sbuf_tensor("p1_h2", [128, HF], BF16))
        h2T = _stk.enter_context(nc.sbuf_tensor("p1_h2T", [128, 2 * HF], BF16))
        pout = _stk.enter_context(nc.sbuf_tensor("p1_po", [128, 2 * D], F32))
        stt = _stk.enter_context(nc.sbuf_tensor("p1_st", [128, 28], F32))
        cc = _stk.enter_context(nc.sbuf_tensor("p1_cc", [128, 2 * 2 * NTB], F32))
        rrow = _stk.enter_context(nc.sbuf_tensor("p1_rr", [1, (2 + 1) * NTB * 128], BF16))
        psf = _stk.enter_context(nc.psum_tensor("p1_ps", [128, 4096], F32))
        blk = _stk.enter_context(nc.Block())
        R = Regs()
        x_tok = [None, None]
        lnq = {}
        wr1_prev = [None]
        part_wr = {}
        last_rks = [[]]

        def bank(i):
            return psf[:, i * 512:(i + 1) * 512]

        def p1_ln_vec(s, tbs=None):
            par = s % 2
            xb_users = []
            for tb in (tbs if tbs is not None else range(NTB)):
                xs = x_tok[par]
                xv = xb[:, tb * D:(tb + 1) * D]
                b1t = kb.emit("vector", lambda v, xv=xv: v.bn_stats(
                    stt[:, 0:6], xv[:, 0:512]), waits=[xs], inc=sv)
                b2t = kb.emit("vector", lambda v, xv=xv: v.bn_stats(
                    stt[:, 6:12], xv[:, 512:1024]), waits=[xs], inc=sv)
                agt = kb.emit("vector", lambda v: v.bn_aggr(
                    stt[:, 12:14], stt[:, 0:12]), waits=[b1t, b2t], inc=sv)
                lnt = kb.emit("scalar", lambda s_: s_.activation(
                    stt[:, 14:15], stt[:, 13:14], AF.Ln, bias=eps_col[:, :]),
                    waits=[agt, teps], inc=ss)
                rst = kb.emit("scalar", lambda s_: s_.activation(
                    stt[:, 15:16], stt[:, 14:15], AF.Exp, scale=-0.5),
                    waits=[lnt], inc=ss)
                nmt = kb.emit("vector", lambda v: v.tensor_scalar(
                    stt[:, 16:17], stt[:, 12:13], stt[:, 15:16], -1.0,
                    ALU.mult, ALU.mult), waits=[rst], inc=sv)
                xnt = kb.emit("scalar", lambda s_, xv=xv: s_.activation(
                    xn[:, :], xv, AF.Identity, bias=stt[:, 16:17],
                    scale=stt[:, 15:16]), waits=[nmt], inc=ss)
                bft = kb.emit("vector", lambda v, tb=tb: v.tensor_copy(
                    xnbf[:, tb * D:(tb + 1) * D], xn[:, :]),
                    waits=[xnt] + R.waits("xnbf"), inc=sv)
                mx = kb.emit("vector", lambda v: v.reduce_max(
                    stt[:, 0:1], xn[:, :], axis=AX,
                    apply_absolute_value=True), waits=[xnt], inc=sv)
                cl = kb.emit("vector", lambda v: v.tensor_scalar(
                    stt[:, 1:2], stt[:, 0:1], EPS, None, ALU.max),
                    waits=[mx], inc=sv)
                rc = kb.emit("vector", lambda v: v.reciprocal(
                    stt[:, 2:3], stt[:, 1:2]), waits=[cl], inc=sv)
                sq = kb.emit("vector", lambda v: v.tensor_scalar(
                    stt[:, 3:4], stt[:, 2:3], QB, None, ALU.mult),
                    waits=[rc], inc=sv)
                dq = kb.emit("vector", lambda v, tb=tb, par=par:
                             v.tensor_tensor(
                                 cc[:, par * 2 * NTB + tb:
                                    par * 2 * NTB + tb + 1],
                                 stt[:, 1:2], dq_b[:, 0:1], ALU.mult),
                             waits=[cl], inc=sv)
                cb = kb.emit("vector", lambda v, tb=tb: v.tensor_copy(
                    colbf[:, tb:tb + 1], stt[:, 2:3]),
                    waits=[rc] + R.waits("colbf"), inc=sv)
                q1 = kb.emit("vector", lambda v: v.tensor_scalar(
                    xn[:, :], xn[:, :], stt[:, 3:4], RC, ALU.mult, ALU.add),
                    waits=[sq, bft], inc=sv)
                q2 = kb.emit("vector", lambda v, tb=tb: v.tensor_scalar(
                    xqbf[:, tb * D:(tb + 1) * D], xn[:, :], RC, None,
                    ALU.subtract), waits=[q1] + R.waits("xqbf"), inc=sv)
                lnq[("v", s, tb)] = (bft, cb, q2, dq)
                xb_users += [b1t, b2t, xnt]
            R.use("xb", R.waits("xb") + xb_users
                  if tbs and tbs[0] else xb_users)

        def p1_ln_tensor(s):
            par = s % 2
            xn_users, xq_users, cb_users = [], [], []
            xnT_guard = []
            for tb in range(NTB):
                bft, cb, q2, dq = lnq[("v", s, tb)]
                bx = (tb % 2) * 4
                rw7 = R.waits("rb%d" % (tb % 2))
                trr = kb.emit("tensor", lambda t, tb=tb, bx=bx: t.matmul(
                    psf[0:1, bx * 512 + 384:bx * 512 + 512],
                    colbf[:, tb:tb + 1],
                    ident[:, :], start=True, stop=True),
                    waits=[cb, tid] + rw7 + R.waits("b%d" % bx), inc=st)
                rslot = par * NTB + tb
                rr = kb.emit("vector", lambda v, rslot=rslot, bx=bx:
                             v.tensor_copy(
                    rrow[:, rslot * 128:(rslot + 1) * 128],
                    psf[0:1, bx * 512 + 384:bx * 512 + 512]),
                    waits=[trr] + R.waits("rrow%d" % rslot), inc=sv)
                R.use("rb%d" % (tb % 2), [rr])
                cb_users.append(trr)
                rw = R.waits("b%d" % bx) + R.waits("b%d" % (bx + 1)) + [rr]
                mts = []
                for c in range(8):
                    mts.append(kb.emit(
                        "tensor", lambda t, c=c, tb=tb, bx=bx: t.matmul(
                            psf[:, bx * 512 + c * 128:
                                bx * 512 + (c + 1) * 128],
                            xnbf[:, tb * D + c * 128:tb * D + (c + 1) * 128],
                            ident[:, :], start=True, stop=True),
                        waits=[bft, tid] + rw,
                        inc=(st if c == 7 else None)))
                mts = mts[-1:]
                xn_users += mts
                odw = ([wr1_prev[0]] if wr1_prev[0] else []) + xnT_guard
                ev_eng = "scalar" if tb % 2 == 0 else "vector"
                ev_sem = ss if tb % 2 == 0 else sv
                if tb % 2 == 0:
                    ev1 = kb.emit("scalar", lambda s_, bx=bx: s_.activation(
                        xnT[:, :], psf[:, bx * 512:bx * 512 + 1024], AF.Copy),
                        waits=mts + odw, inc=ss)
                else:
                    ev1 = kb.emit("vector", lambda v, bx=bx: v.tensor_copy(
                        xnT[:, :], psf[:, bx * 512:bx * 512 + 1024]),
                        waits=mts + odw, inc=sv)
                xnT_guard = [ev1]
                R.use("b%d" % bx, [ev1]); R.use("b%d" % (bx + 1), [ev1])
                wr1 = kb.emit("scalar", lambda s_, s=s, tb=tb: s_.dma_start(
                    ap(xn_sc, s * P1S + tb * 128,
                       [[S, 128], [128 * S, 8], [1, 128]]),
                    ap(xnT, 0, [[8 * 128, 128], [128, 8], [1, 128]])),
                    waits=[ev1], inc=od_xn, inc_k=16)
                wr1_prev[0] = wr1
                rw = R.waits("b%d" % (bx + 2)) + R.waits("b%d" % (bx + 3))
                mts2 = []
                for c in range(8):
                    mts2.append(kb.emit(
                        "tensor", lambda t, c=c, tb=tb, bx=bx: t.matmul(
                            psf[:, (bx + 2) * 512 + c * 128:
                                (bx + 2) * 512 + (c + 1) * 128],
                            xqbf[:, tb * D + c * 128:tb * D + (c + 1) * 128],
                            ident[:, :], start=True, stop=True),
                        waits=[q2, tid] + rw,
                        inc=(st if c == 7 else None)))
                mts2 = mts2[-1:]
                xq_users += mts2
                dst = ap(xqT, par * 8 * P1S + tb * 1024,
                         [[2 * 8 * P1S, 128], [1, 1024]])
                if tb % 2 == 0:
                    ev2 = kb.emit("vector", lambda v, dst=dst, bx=bx:
                                  v.tensor_copy(dst, ap(
                                      psf, (bx + 2) * 512,
                                      [[4096, 128], [1, 1024]])),
                                  waits=mts2, inc=sv)
                else:
                    ev2 = kb.emit("scalar", lambda s_, dst=dst, bx=bx:
                                  s_.activation(dst, ap(
                                      psf, (bx + 2) * 512,
                                      [[4096, 128], [1, 1024]]), AF.Copy),
                                  waits=mts2, inc=ss)
                R.use("b%d" % (bx + 2), [ev2]); R.use("b%d" % (bx + 3), [ev2])
                lnq[(s, tb)] = (ev2, dq, rr, wr1)
            R.use("xnbf", xn_users)
            R.use("xqbf", xq_users)
            R.use("colbf", cb_users)

        def p1_ffn1(s):
            par = s % 2
            gelu = {}
            mm_last = {}
            for hc in range(8):
                for tb in range(NTB):
                    bi = (hc % 2) * 4 + tb
                    ev2, dq, rr, _ = lnq[(s, tb)]
                    rw = R.waits("b%d" % bi)
                    rk = kb.emit(
                        "tensor", lambda t, bi=bi, tb=tb, par=par, hc=hc:
                        t.matmul(bank(bi),
                                 rrow[:, (par * NTB + tb) * 128:
                                      (par * NTB + tb + 1) * 128],
                                 b1s_row[:, hc * 512:(hc + 1) * 512],
                                 start=True, stop=False),
                        waits=[rr] + rw, inc=st)
                    R.use("rrow%d" % (par * NTB + tb),
                          R.waits("rrow%d" % (par * NTB + tb)) + [rk])
                    last = rk
                    for di in range(8):
                        last = kb.emit(
                            "tensor",
                            lambda t, bi=bi, di=di, hc=hc, par=par, tb=tb:
                            t.matmul(bank(bi),
                                     xqT[:, par * 8 * P1S + tb * 1024 +
                                         di * 128:par * 8 * P1S +
                                         tb * 1024 + di * 128 + 128],
                                     w1res[:, di * HF + hc * 512:
                                           di * HF + (hc + 1) * 512],
                                     start=False, stop=(di == 7)),
                            waits=[ev2], inc=(st if di == 7 else None))
                    mm_last[hc] = last
                    g = kb.emit("scalar", lambda s_, bi=bi, tb=tb, hc=hc:
                                s_.activation(
                                    hact[:, tb * HF + hc * 512:
                                         tb * HF + (hc + 1) * 512],
                                    bank(bi), AF.Gelu,
                                    scale=cc[:, (par * 2 * NTB) + tb:
                                             (par * 2 * NTB) + tb + 1]),
                                waits=[last, dq], inc=ss)
                    gelu[(hc, tb)] = g
                    R.use("b%d" % bi, [g])
                    am = kb.emit("vector", lambda v, tb=tb, hc=hc:
                                 v.reduce_max(
                                     stt[:, 26:27],
                                     hact[:, tb * HF + hc * 512:
                                          tb * HF + (hc + 1) * 512],
                                     axis=AX, apply_absolute_value=True),
                                 waits=[g] + R.waits("amx"), inc=sv)
                    am2 = kb.emit("vector", lambda v, tb=tb, hc=hc:
                                  (v.tensor_copy(stt[:, 22 + tb:23 + tb],
                                                 stt[:, 26:27]) if hc == 0
                                   else
                                   v.tensor_tensor(stt[:, 22 + tb:23 + tb],
                                                   stt[:, 22 + tb:23 + tb],
                                                   stt[:, 26:27], ALU.max)),
                                  waits=[am], inc=sv)
                    R.use("amx", [am2])
            amx = {tb: (sv, sv.n) for tb in range(NTB)}
            return gelu, amx

        def p1_quant_h(s, gelu, amx, tbs, h2ev, trr_bank):
            for tb in tbs:
                cl = kb.emit("vector", lambda v, tb=tb: v.tensor_scalar(
                    stt[:, 18:19], stt[:, 22 + tb:23 + tb], EPS, None,
                    ALU.max), waits=[amx[tb]], inc=sv)
                rc = kb.emit("vector", lambda v: v.reciprocal(
                    stt[:, 19:20], stt[:, 18:19]), waits=[cl], inc=sv)
                sq = kb.emit("vector", lambda v: v.tensor_scalar(
                    stt[:, 20:21], stt[:, 19:20], QB, None, ALU.mult),
                    waits=[rc], inc=sv)
                dq = kb.emit("vector", lambda v, tb=tb: v.tensor_tensor(
                    cc[:, (s % 2) * 2 * NTB + NTB + tb:
                       (s % 2) * 2 * NTB + NTB + tb + 1],
                    stt[:, 18:19], dq_b[:, 1:2], ALU.mult), waits=[cl], inc=sv)
                cb = kb.emit("vector", lambda v: v.tensor_copy(
                    colbf2[:, :], stt[:, 19:20]), waits=[rc], inc=sv)
                rw = R.waits("b%d" % trr_bank)
                trr = kb.emit("tensor", lambda t, trr_bank=trr_bank: t.matmul(
                    psf[0:1, trr_bank * 512:trr_bank * 512 + 128],
                    colbf2[:, :], ident[:, :], start=True, stop=True),
                    waits=[cb, tid] + rw, inc=st)
                rr = kb.emit("vector", lambda v, tb=tb, trr_bank=trr_bank:
                             v.tensor_copy(
                    rrow[:, (2 * NTB + tb) * 128:(2 * NTB + tb + 1) * 128],
                    psf[0:1, trr_bank * 512:trr_bank * 512 + 128]),
                    waits=[trr] + R.waits("rq%d" % tb), inc=sv)
                R.use("b%d" % trr_bank, [rr])
                rnds = []
                h2war = R.waits("h2tb")
                for hh in range(4):
                    r1 = kb.emit("vector", lambda v, tb=tb, hh=hh:
                                 v.tensor_scalar(
                                     rtmp[:, :],
                                     hact[:, tb * HF + hh * 1024:
                                          tb * HF + (hh + 1) * 1024],
                                     stt[:, 20:21], RC, ALU.mult, ALU.add),
                                 waits=[sq] + rnds[-1:] +
                                 [gelu[(hc, tb)] for hc in range(8)], inc=sv)
                    r2 = kb.emit("vector", lambda v, hh=hh: v.tensor_scalar(
                        h2tb[:, hh * 1024:(hh + 1) * 1024], rtmp[:, :],
                        RC, None, ALU.subtract),
                        waits=[r1] + h2war, inc=sv)
                    rnds.append(r2)
                for hb in range(4):
                    bi = hb % 2
                    rw = R.waits("b%d" % bi)
                    mts = []
                    for c in range(8):
                        hc = hb * 8 + c
                        mts.append(kb.emit(
                            "tensor", lambda t, hc=hc, c=c, bi=bi: t.matmul(
                                psf[:, bi * 1024 + c * 128:
                                    bi * 1024 + (c + 1) * 128],
                                h2tb[:, hc * 128:(hc + 1) * 128],
                                ident[:, :], start=True, stop=True),
                            waits=rnds[hb:hb + 1] + [tid] + rw,
                            inc=(st if c == 7 else None)))
                    mts = mts[-1:]
                    dst = ap(h2T, (tb % 2) * HF + hb * 1024,
                             [[2 * HF, 128], [1, 1024]])
                    ev = kb.emit("scalar", lambda s_, dst=dst, bi=bi:
                                 s_.activation(dst, ap(
                                     psf, bi * 1024,
                                     [[4096, 128], [1, 1024]]), AF.Copy),
                                 waits=mts + R.waits("h2s%d" % (tb % 2)),
                                 inc=ss)
                    R.use("b%d" % bi, [ev])
                    h2ev[(tb, hb)] = ev
                R.use("h2tb", [h2ev[(tb, hb)] for hb in range(4)])
                R.use("amx", R.waits("amx") + [cl])
                h2ev[("dq", tb)] = dq
                h2ev[("rr", tb)] = rr
            return h2ev

        def p1_ffn2_pass(s, h2ev, pass_, w2seq):
            par = s % 2
            base_x = par * NTB * D
            tbs = [2 * pass_, 2 * pass_ + 1]
            bank0 = 4 * (1 - pass_)

            def bi_of(tb, dh):
                return bank0 + (tb % 2) * 2 + dh

            rks = {}
            for tb in tbs:
                for dh in range(2):
                    bi = bi_of(tb, dh)
                    rw = R.waits("b%d" % bi)
                    rks[(tb, dh)] = kb.emit(
                        "tensor", lambda t, bi=bi, tb=tb, dh=dh: t.matmul(
                            bank(bi), rrow[:, (2 * NTB + tb) * 128:
                                           (2 * NTB + tb + 1) * 128],
                            b2s_row[:, dh * 512:(dh + 1) * 512],
                            start=True, stop=False,
                            skip_group_check=True),
                        waits=[h2ev[("rr", tb)]] + rw, inc=st)
                R.use("rq%d" % tb, [rks[(tb, 0)], rks[(tb, 1)]])
            mm_last = {}
            for ck in range(NHC // 4):
                last = None
                for hq in range(4):
                    hc = ck * 4 + hq
                    hb = hc // 8
                    for tb in tbs:
                        for dh in range(2):
                            bi = bi_of(tb, dh)
                            last = kb.emit(
                                "tensor",
                                lambda t, bi=bi, hc=hc, tb=tb, dh=dh:
                                t.matmul(bank(bi),
                                         h2T[:, (tb % 2) * HF + hc * 128:
                                             (tb % 2) * HF + hc * 128 + 128],
                                         w2res[:, hc * D + dh * 512:
                                               hc * D + (dh + 1) * 512],
                                         start=False, stop=(hc == NHC - 1),
                                         skip_group_check=True),
                                waits=[h2ev[(tb, hb)],
                                       rks[(tb, dh)]],
                                inc=(st if (dh == 1 and tb == tbs[-1])
                                     else None))
                    mm_last[hc] = last
            for tb in tbs:
                for dh in range(2):
                    R.use("b%d" % bi_of(tb, dh), [mm_last[NHC - 1]])
                R.use("h2s%d" % (tb % 2), [mm_last[NHC - 1]])
            return (tbs, bi_of, mm_last[NHC - 1], h2ev)

        def p1_ffn2_evac(s, state):
            par = s % 2
            tbs, bi_of, mm_tok, h2ev = state
            evs_all = []
            for tb in tbs:
                po = tb % 2
                pw = [part_wr[po]] if po in part_wr else []
                for dh in range(2):
                    bi = bi_of(tb, dh)
                    ev = kb.emit(
                        "vector",
                        lambda v, bi=bi, tb=tb, dh=dh, po=po:
                        v.tensor_scalar(
                            pout[:, po * D + dh * 512:
                                 po * D + (dh + 1) * 512],
                            bank(bi),
                            cc[:, par * 2 * NTB + NTB + tb:
                               par * 2 * NTB + NTB + tb + 1],
                            None, ALU.mult),
                        waits=[mm_tok, h2ev[("dq", tb)]] + pw, inc=sv)
                    R.use("b%d" % bi, [ev])
                    evs_all.append(ev)
                part_wr[po] = kb.emit(
                    "scalar", lambda s_, s=s, tb=tb, po=po: s_.dma_start(
                        part_sc.ap()[s * P1S + tb * 128:
                                     s * P1S + (tb + 1) * 128, :],
                        pout[:, po * D:(po + 1) * D]),
                    waits=evs_all[-2:], inc=od_part[po], inc_k=16)
            return evs_all

        prev_evs = {0: [], 1: []}

        def emit_xload(s):
            par = s % 2
            x_tok[par] = kb.emit("sync", lambda s_, s=s: s_.dma_start(
                ap(xb, 0, [[NTB * D, 128], [D, NTB], [1, D]]),
                ap(x_in, s * P1S * D, [[D, 128], [128 * D, NTB], [1, D]])),
                waits=R.waits("xb"), inc=sx[par], inc_k=16)

        emit_xload(0)
        p1_ln_vec(0, [0, 1])
        p1_ln_vec(0, [2, 3])
        p1_ln_tensor(0)
        w2seq = {"mm": {}, "n": 0}
        for s in range(N1):
            par = s % 2
            if s + 1 < N1:
                emit_xload(s + 1)
            gelu, amx = p1_ffn1(s)
            h2ev = {}
            p1_quant_h(s, gelu, amx, [0, 1], h2ev, 7)
            st_a = p1_ffn2_pass(s, h2ev, 0, w2seq)
            if s + 1 < N1:
                p1_ln_vec(s + 1, [0, 1])
            p1_quant_h(s, gelu, amx, [2, 3], h2ev, 3)
            ev_a = p1_ffn2_evac(s, st_a)
            st_b = p1_ffn2_pass(s, h2ev, 1, w2seq)
            if s + 1 < N1:
                p1_ln_vec(s + 1, [2, 3])
            ev_b = p1_ffn2_evac(s, st_b)
            if s + 1 < N1:
                p1_ln_tensor(s + 1)
            prev_evs[par] = ev_a + ev_b
        kb.emit("sync", lambda s: s.nop(),
                waits=[(od_xn, od_xn.n)] +
                [(p, p.n) for p in od_part])
        kb.flush(blk)
    kb.barrier_reached()
    _stk01.close()

    k_T = nc.alloc_sbuf_tensor("k_T", [128, NH * S], BF16)
    v_eff = nc.alloc_sbuf_tensor("v_eff", [128, KC * 4 * 129], BF16)
    with ExitStack() as _stk:
        xnT2 = _stk.enter_context(nc.sbuf_tensor("p2_xnT", [128, 4 * S], BF16))
        wbuf = _stk.enter_context(nc.sbuf_tensor("p2_w", [128, 8 * D], BF16))
        wbuf2 = _stk.enter_context(nc.sbuf_tensor("p2_w2", [128, 8 * D], BF16))
        wvb = _stk.enter_context(nc.sbuf_tensor("p2_wv", [128, 8 * 512], BF16))
        qstg = _stk.enter_context(nc.sbuf_tensor("p2_qs", [128, 6 * 512], BF16))
        bcol = _stk.enter_context(nc.sbuf_tensor("p2_bc", [128, 16], F32))
        brow2 = _stk.enter_context(nc.sbuf_tensor("p2_br", [1, D], F32))
        bvrow = _stk.enter_context(nc.sbuf_tensor("p2_bvr", [1, 512], BF16))
        ps2 = _stk.enter_context(nc.psum_tensor("p2_ps", [128, 4096], F32))
        blk = _stk.enter_context(nc.Block())
        xn_toks = {}

        def load_xn(tt_, war):
            xn_toks[tt_] = kb.emit(
                "sync", lambda s, tt_=tt_: s.dma_start(
                    ap(xnT2, (tt_ % max(1, S // 1024)) * 512,
                       [[4 * S, 128], [S // 2, 8], [1, 512]]),
                    ap(xn_sc, tt_ * 512,
                       [[S, 128], [128 * S, 8], [1, 512]])),
                waits=[(od_xn, od_xn.n)] + war, inc=sxn, inc_k=16)

        for tt_ in range(S // 1024):
            load_xn(tt_, [])
        wq_tok = kb.emit("gpsimd", lambda g: g.dma_start(
            ap(wbuf2, 0, [[8 * D, 128], [D, 8], [1, D]]),
            ap(wq_in, 0, [[D, 128], [128 * D, 8], [1, D]])),
            inc=gd2, inc_k=16)
        t_b = kb.emit("sync", lambda s: s.dma_start(
            brow2[:, :], bq_in.ap()[:]), inc=sbr, inc_k=16)
        bq_toks = []
        for c in range(8):
            mm = kb.emit("tensor", lambda t, c=c: t.matmul(
                ps2[:, 3584 + c:3584 + c + 1],
                brow2[:, c * 128:(c + 1) * 128], ones_col_f[0:1, 0:1],
                start=True, stop=True), waits=[t_b, tonesc], inc=st)
            bq_toks.append(mm)
        t_bqc = kb.emit("vector", lambda v: v.tensor_copy(
            bcol[:, 0:8], ps2[:, 3584:3592]), waits=bq_toks, inc=sv)
        t_b2 = kb.emit("sync", lambda s: s.dma_start(
            brow2[:, :], bk_in.ap()[:]), waits=bq_toks, inc=sbr, inc_k=16)
        bk_toks = []
        for c in range(8):
            mm = kb.emit("tensor", lambda t, c=c: t.matmul(
                ps2[:, 3584 + c:3584 + c + 1],
                brow2[:, c * 128:(c + 1) * 128], ones_col_f[0:1, 0:1],
                start=True, stop=True), waits=[t_b2, t_bqc], inc=st)
            bk_toks.append(mm)
        t_bkc = kb.emit("vector", lambda v: v.tensor_copy(
            bcol[:, 8:16], ps2[:, 3584:3592]), waits=bk_toks, inc=sv)
        tlv = kb.emit("gpsimd", lambda g: g.dma_start(
            ap(wbuf, 0, [[8 * D, 128], [D, 8], [1, D]]),
            ap(wv_in, 0, [[D, 128], [128 * D, 8], [1, D]])),
            inc=gd, inc_k=16)
        wv_toks = []
        for di in range(8):
            tc = kb.emit("vector", lambda v, di=di: v.scalar_tensor_tensor(
                wvb[:, di * 512:(di + 1) * 512],
                wbuf[:, di * D + 512:(di + 1) * D],
                nlam_b[:, :], wbuf[:, di * D:di * D + 512],
                ALU.mult, ALU.add), waits=[tlv, nlmb], inc=sv)
            wv_toks.append(tc)
        t_bv = kb.emit("sync", lambda s: s.dma_start(
            brow2[:, :], bv_in.ap()[:]), waits=bk_toks, inc=sbr, inc_k=16)
        t_bv2 = kb.emit("vector", lambda v: v.scalar_tensor_tensor(
            bvrow[:, :], brow2[:, 512:1024], nlam_b[0:1, 0:1],
            brow2[:, 0:512], ALU.mult, ALU.add), waits=[t_bv, nlmb], inc=sv)

        NTT = S // 512
        q_wr = []

        def proj(w_in, dst_fn, wload_waits, tts, bank_war=None, wb=None,
                 preload=None):
            if wb is None:
                wb = wbuf
            if preload is None:
                preload = kb.emit("gpsimd", lambda g, wb=wb: g.dma_start(
                    ap(wb, 0, [[8 * D, 128], [D, 8], [1, D]]),
                    ap(w_in, 0, [[D, 128], [128 * D, 8], [1, D]])),
                    waits=wload_waits, inc=gd, inc_k=16)
            wtoks = [preload] * 8
            out_toks = []
            last_mm = []
            nt = len(tts)
            for do in range(8):
                for ti, tt in enumerate(tts):
                    if do:
                        war = [out_toks[(do - 1) * nt + ti]]
                    else:
                        war = [bank_war[ti]] if bank_war else []
                    last = None
                    for di in range(8):
                        w0 = [t_bkc] if (do == 0 and di == 0) else []
                        last = kb.emit(
                            "tensor", lambda t, do=do, di=di, tt=tt, wb=wb:
                            t.matmul(ps2[:, tt * 512:(tt + 1) * 512],
                                     wb[:, di * D + do * 128:
                                        di * D + do * 128 + 128],
                                     xnT2[:, di * (S // 2) +
                                          (tt % max(1, S // 1024)) * 512:
                                          di * (S // 2) +
                                          (tt % max(1, S // 1024)) * 512 +
                                          512],
                                     start=(di == 0), stop=(di == 7)),
                            waits=[wtoks[di], xn_toks[tt]] + war + w0,
                            inc=(st if di == 7 else None))
                    last_mm.append(last)
                    out_toks.append(dst_fn(do, tt, [last]))
            return out_toks, last_mm, preload

        def q_dst(do, tt, waits):
            par = len(q_wr) % 6
            ev = kb.emit("scalar", lambda s_, do=do, tt=tt, par=par:
                         s_.activation(
                             qstg[:, par * 512:(par + 1) * 512],
                             ps2[:, tt * 512:(tt + 1) * 512],
                             AF.Identity, bias=bcol[:, do:do + 1]),
                         waits=waits + [t_bqc] +
                         ([q_wr[-6]] if len(q_wr) >= 6 else []),
                         inc=ss)
            wr = kb.emit("sync", lambda s, do=do, tt=tt, par=par: s.dma_start(
                ap(q_sc, do * 128 * S + tt * 512, [[S, 128], [1, 512]]),
                qstg[:, par * 512:(par + 1) * 512]), waits=[ev],
                inc=od_q[par], inc_k=16)
            q_wr.append(wr)
            return ev

        def k_dst(do, tt, waits):
            return kb.emit("scalar", lambda s_, do=do, tt=tt: s_.activation(
                k_T[:, do * S + tt * 512:do * S + (tt + 1) * 512],
                ps2[:, tt * 512:(tt + 1) * 512],
                AF.Identity, bias=bcol[:, 8 + do:9 + do]),
                waits=waits + [t_bkc], inc=ss)

        q0, q_mm0, wq_t = proj(wq_in, q_dst, [], list(range(NTT // 2)),
                               wb=wbuf2, preload=wq_tok)
        k0, k_mm0, wk_t = proj(wk_in, k_dst, wv_toks,
                               list(range(NTT // 2)),
                               bank_war=q0[-(NTT // 2):])

        t_vo = kb.emit("vector", lambda v: v.memset(
            ap(v_eff, 128, [[KC * 4 * 129, 128], [129, KC * 4], [1, 1]]),
            1.0), inc=sv)
        vev = []

        def v_chunk(kc, guard):
            bi = kc % 4
            po = ps2[:, bi * 512:(bi + 1) * 512]
            war = []
            if kc >= 4:
                war.append(vev[kc - 4])
            if (kc % (KC // 2)) < 4 and bi in guard:
                war.append(guard[bi])
            rk = kb.emit("tensor", lambda t, po=po: t.matmul(
                po, ones_row[:, :], bvrow[:, :], start=True, stop=False),
                waits=[t_bv2, tones] + war, inc=st)
            last = rk
            for di in range(8):
                last = kb.emit("tensor", lambda t, po=po, di=di, kc=kc:
                               t.matmul(po,
                                        xnT2[:, di * (S // 2) +
                                             (kc % (KC // 2)) * 128:
                                             di * (S // 2) +
                                             (kc % (KC // 2)) * 128 + 128],
                                        wvb[:, di * 512:(di + 1) * 512],
                                        start=False, stop=(di == 7)),
                               waits=[wv_toks[di], xn_toks[kc // 4]],
                               inc=(st if di == 7 else None))
            dst = ap(v_eff, kc * 4 * 129,
                     [[KC * 4 * 129, 128], [129, 4], [1, 128]])
            src = ap(ps2, bi * 512, [[4096, 128], [128, 4], [1, 128]])
            vev.append(kb.emit("vector", lambda v, dst=dst, src=src:
                               v.tensor_copy(dst, src),
                               waits=[last, t_vo], inc=sv))

        guard0 = {tt: k0[-(NTT // 2) + ti]
                  for ti, tt in enumerate(range(NTT // 2)) if tt < 4}
        for kc in range(KC // 2):
            v_chunk(kc, guard0)
        for tt_ in range(NTT // 2, NTT):
            load_xn(tt_, [vev[(tt_ - NTT // 2 + 1) * 4 - 1]])
        q1, _, _ = proj(wq_in, q_dst, [], list(range(NTT // 2, NTT)),
                        wb=wbuf2, preload=wq_t)
        k1, _, _ = proj(wk_in, k_dst, [], list(range(NTT // 2, NTT)),
                        bank_war=q1[-(NTT - NTT // 2):], preload=wk_t)
        nt1 = NTT - NTT // 2
        guard1 = {tt: k1[-nt1 + ti]
                  for ti, tt in enumerate(range(NTT // 2, NTT)) if tt < 4}
        for kc in range(KC // 2, KC):
            v_chunk(kc, guard1)
        kb.emit("sync", lambda s: s.nop(),
                waits=[(q, q.n) for q in od_q])
        kb.flush(blk)
    kb.barrier_reached()

    with ExitStack() as _stk:
        wo = _stk.enter_context(nc.sbuf_tensor("p3_wo", [128, 8 * D], BF16))
        qb3 = _stk.enter_context(nc.sbuf_tensor("p3_q", [128, 2 * NH * P3S], BF16))
        expT = _stk.enter_context(nc.sbuf_tensor("p3_e", [128, 2 * KC * P3S], BF16))
        attn = _stk.enter_context(nc.sbuf_tensor("p3_a", [128, 4 * D], BF16))
        zbf = _stk.enter_context(nc.sbuf_tensor("p3_z", [128, D], BF16))
        lbc = _stk.enter_context(nc.sbuf_tensor("p3_lbc", [128, 8], BF16))
        nT = _stk.enter_context(nc.sbuf_tensor("p3_nT", [128, 8 * P3S], BF16))
        partb = _stk.enter_context(nc.sbuf_tensor("p3_pp", [128, 2 * NQB * D], F32))
        xb3 = _stk.enter_context(nc.sbuf_tensor("p3_x", [128, NQB * D], F32))
        outb = _stk.enter_context(nc.sbuf_tensor("p3_o", [128, 2 * D], F32))
        st3 = _stk.enter_context(nc.sbuf_tensor("p3_st", [128, 24], F32))
        lncols = _stk.enter_context(nc.sbuf_tensor("p3_ln", [128, 16], F32))
        borow = _stk.enter_context(nc.sbuf_tensor("p3_bo", [1, D], BF16))
        brow3 = _stk.enter_context(nc.sbuf_tensor("p3_br", [1, 512], F32))
        wstg3 = _stk.enter_context(nc.sbuf_tensor("p3_stg", [128, 512], F32))
        wraw = _stk.enter_context(nc.sbuf_tensor("p3_wraw", [128, 512], BF16))
        pqk = _stk.enter_context(nc.psum_tensor("p3_qk", [128, 3072], F32))
        ppv = _stk.enter_context(nc.psum_tensor("p3_pv", [128, 1024], F32))
        pms = pqk
        blk = _stk.enter_context(nc.Block())
        R3 = Regs()
        def row_to_cols(v_in, col_off, prevs):
            toks = []
            for hf in range(2):
                tl = kb.emit("sync", lambda s, hf=hf: s.dma_start(
                    brow3[:, :], v_in.ap()[hf * 512:(hf + 1) * 512]),
                    waits=prevs + toks[-1:], inc=spro, inc_k=16)
                for c in range(4):
                    toks.append(kb.emit(
                        "tensor", lambda t, c=c, hf=hf, col_off=col_off:
                        t.matmul(ppv[:, col_off + hf * 4 + c:
                                     col_off + hf * 4 + c + 1],
                                 brow3[:, c * 128:(c + 1) * 128],
                                 ones_col_f[0:1, 0:1], start=True, stop=True),
                        waits=[tl, tonesc], inc=st))
            return toks

        lw_toks = row_to_cols(lnw_in, 0, [])
        t_lwc = kb.emit("vector", lambda v: v.tensor_scalar(
            lncols[:, 0:8], ppv[:, 0:8], P02, None, ALU.mult),
            waits=lw_toks, inc=sv)
        lb_toks = row_to_cols(lnb_in, 8, lw_toks + [t_lwc])
        t_lbc = kb.emit("vector", lambda v: v.tensor_scalar(
            lncols[:, 8:16], ppv[:, 8:16], P02, None, ALU.mult),
            waits=lb_toks, inc=sv)
        t_lbb = kb.emit("vector", lambda v: v.tensor_copy(
            lbc[:, 0:8], lncols[:, 8:16]), waits=[t_lbc], inc=sv)
        t_bo2 = None
        for hf in range(2):
            t_bo = kb.emit("sync", lambda s, hf=hf: s.dma_start(
                brow3[:, :], bo_in.ap()[hf * 512:(hf + 1) * 512]),
                waits=lb_toks + ([t_bo2] if t_bo2 else []), inc=spro,
                inc_k=16)
            t_bo2 = kb.emit("vector", lambda v, hf=hf: v.tensor_copy(
                borow[:, hf * 512:(hf + 1) * 512], brow3[:, :]),
                waits=[t_bo], inc=sv)
        bo_mms = {0: [], 1: []}
        for dh in range(2):
            bo_mms[dh].append(kb.emit("tensor", lambda t, dh=dh: t.matmul(
                ppv[0:1, dh * 512:dh * 512 + 512], ones_row[0:1, 0:1],
                borow[:, dh * 512:(dh + 1) * 512], start=True, stop=False,
                skip_group_check=True),
                waits=[t_bo2, tones, t_lbc], inc=st))
        wo_toks8 = {}
        prev_parts = {}
        prev_tw = {}
        for di in range(8):
            for dh in range(2):
                i = di * 2 + dh
                stg_ = wstg3 if i % 2 == 0 else outb
                pw = [prev_parts.get(i - 2, t_bo2), prev_tw.get(i - 2)]
                tl = kb.emit("sync", lambda s, di=di, dh=dh, stg_=stg_:
                             s.dma_start(
                    stg_[:, 0:512],
                    wo_in.ap()[di * 128:(di + 1) * 128,
                               dh * 512:(dh + 1) * 512]),
                    waits=pw, inc=(spro if i % 2 == 0 else od_w),
                    inc_k=16)
                tr = kb.emit("vector", lambda v, stg_=stg_: v.tensor_copy(
                    wraw[:, :], stg_[:, 0:512]),
                    waits=[tl, prev_parts.get(i - 1)], inc=sv)
                tw = kb.emit("vector",
                             lambda v, di=di, dh=dh, stg_=stg_:
                             v.tensor_scalar(
                    wo[:, di * D + dh * 512:di * D + (dh + 1) * 512],
                    stg_[:, 0:512], lncols[:, di:di + 1],
                    None, ALU.mult),
                    waits=[tl, t_lwc], inc=sv)
                mm = kb.emit("tensor", lambda t, di=di, dh=dh: t.matmul(
                    ppv[0:1, dh * 512:dh * 512 + 512], lbc[:, di:di + 1],
                    wraw[:, :], start=False,
                    stop=(di == 7), skip_group_check=True),
                    waits=[tr, t_lbb, bo_mms[dh][-1]], inc=st)
                bo_mms[dh].append(mm)
                prev_parts[i] = mm
                prev_tw[i] = tw
                wo_toks8[(di, dh)] = tw
        wo_toks = [wo_toks8[(di, 1)] for di in range(8)]
        t_bop = kb.emit("vector", lambda v: v.tensor_copy(
            borow[:, :], ppv[0:1, 0:1024]),
            waits=[bo_mms[0][-1], bo_mms[1][-1]], inc=sv)
        R3.use("pv0", [t_bop])
        R3.use("pv1", [t_bop])

        q_tok = [None, None]
        p_tok = [None, None]
        x_tok3 = [None]
        outwr = []
        prev_exp = {0: [], 1: [], 2: []}
        last_pv = [None]
        all_evs = {}
        qk_mms = {}

        def emit_loads(s3):
            spar = s3 % 2
            q_tok[spar] = kb.emit("sync", lambda s_, s3=s3, spar=spar:
                                  s_.dma_start(
                ap(qb3, spar * NH * P3S,
                   [[2 * NH * P3S, 128], [P3S, NH], [1, P3S]]),
                ap(q_sc, s3 * P3S, [[S, 128], [128 * S, NH], [1, P3S]])),
                waits=[(q, q.n) for q in od_q] +
                R3.waits("qb%d" % spar), inc=sq3[spar], inc_k=16)
            p_tok[spar] = kb.emit("sync", lambda s_, s3=s3, spar=spar:
                                  s_.dma_start(
                ap(partb, spar * NQB * D,
                   [[2 * NQB * D, 128], [D, NQB], [1, D]]),
                ap(part_sc, s3 * P3S * D, [[D, 128], [128 * D, NQB], [1, D]])),
                waits=[(p, p.n) for p in od_part] + R3.waits("pb%d" % spar),
                inc=sp3[spar], inc_k=16)


        def pv_sub(pu, g):
            ps3, ph, pexp = pu
            gi = ps3 * NH + ph
            epar = gi % 2
            ebase = epar * KC * P3S
            c = ph % 4
            toks = []
            for qbi in range(NQB):
                slot = ((gi * NQB + qbi) % 2) * 512
                rw = R3.waits("pv%d" % (slot // 512)) if g == 0 else []
                for j in range(4):
                    kc = g * 4 + j
                    toks.append(kb.emit(
                        "tensor",
                        lambda t, kc=kc, qbi=qbi, slot=slot, c=c, ebase=ebase:
                        t.matmul(ppv[:, slot:slot + 129],
                                 expT[:, ebase + kc * P3S + qbi * 128:
                                      ebase + kc * P3S + qbi * 128 + 128],
                                 v_eff[:, kc * 4 * 129 + c * 129:
                                       kc * 4 * 129 + (c + 1) * 129],
                                 start=(kc == 0), stop=(kc == KC - 1),
                                 skip_group_check=True),
                        waits=[pexp[g]] + rw,
                        inc=(st if j == 3 else None)))
            R3.use("e%d_%d" % (epar, g), [toks[-1]])
            return toks[-1]

        def pv_finish(pu, last_tok):
            ps3, ph, pexp = pu
            apar = ps3 % 2
            gi = ps3 * NH + ph
            epar = gi % 2
            pv_evs = []
            for qbi in range(NQB):
                slot = ((gi * NQB + qbi) % 2) * 512
                rc = kb.emit("vector", lambda v, slot=slot, qbi=qbi:
                             v.reciprocal(st3[:, qbi:qbi + 1],
                                          ppv[:, slot + 128:slot + 129]),
                             waits=[last_tok], inc=sv)
                aw = R3.waits("at%d_%d" % (qbi, apar)) if ph == 0 else []
                ev = kb.emit("vector",
                             lambda v, slot=slot, ph=ph, qbi=qbi, apar=apar:
                             v.tensor_scalar(
                                 attn[:, apar * 2 * D + qbi * D + ph * 128:
                                      apar * 2 * D + qbi * D +
                                      (ph + 1) * 128],
                                 ppv[:, slot:slot + 128],
                                 st3[:, qbi:qbi + 1], None, ALU.mult),
                             waits=[rc] + aw, inc=sv)
                R3.use("pv%d" % (slot // 512), [ev])
                pv_evs.append(ev)
            for qbi in range(NQB):
                all_evs.setdefault(ps3, {}).setdefault(qbi, []).append(
                    pv_evs[qbi])

        def emit_unit(s3, h, pend):
            spar = s3 % 2
            gi = s3 * NH + h
            epar = gi % 2
            ebase = epar * KC * P3S
            exp_toks = []
            mms = None
            for g in range(KC // 4):
                gpar = g % 3
                mms = []
                for j in range(4):
                    kc = g * 4 + j
                    mms.append(kb.emit(
                        "tensor",
                        lambda t, h=h, kc=kc, gpar=gpar, spar=spar:
                        t.matmul(pqk[:, gpar * 1024 + (kc % 4) * P3S:
                                     gpar * 1024 + (kc % 4 + 1) * P3S],
                                 k_T[:, h * S + kc * 128:
                                     h * S + (kc + 1) * 128],
                                 qb3[:, spar * NH * P3S + h * P3S:
                                     spar * NH * P3S + (h + 1) * P3S],
                                 start=True, stop=True),
                        waits=[q_tok[spar]] +
                        (exp_toks[g - 3:g - 2] if g >= 3
                         else prev_exp[gpar]),
                        inc=(st if j == 3 else None)))
                mms = mms[-1:]
                ex = kb.emit("scalar",
                             lambda s_, g=g, gpar=gpar, ebase=ebase:
                             s_.activation(
                                 expT[:, ebase + g * 4 * P3S:
                                      ebase + (g + 1) * 4 * P3S],
                                 pqk[:, gpar * 1024:(gpar + 1) * 1024],
                                 AF.Exp, scale=INV_SQRT_HD),
                             waits=mms + R3.waits("e%d_%d" % (epar, g)),
                             inc=ss)
                exp_toks.append(ex)
                if pend is not None:
                    last_pv[0] = pv_sub(pend, g)
            for g, ex in enumerate(exp_toks):
                prev_exp[g % 3] = [ex]
            qk_mms.setdefault(s3, []).append(mms[-1])
            if pend is not None:
                pv_finish(pend, last_pv[0])
            return exp_toks

        def emit_lnwo(s3):
            spar = s3 % 2
            R3.use("qb%d" % spar, qk_mms[s3])
            apar = s3 % 2
            for qbi in range(NQB):
                av = attn[:, apar * 2 * D + qbi * D:
                          apar * 2 * D + (qbi + 1) * D]
                wa = all_evs[s3][qbi]
                b1t = kb.emit("vector", lambda v, av=av: v.bn_stats(
                    st3[:, 4:10], av[:, 0:512]), waits=wa, inc=sv)
                b2t = kb.emit("vector", lambda v, av=av: v.bn_stats(
                    st3[:, 10:16], av[:, 512:1024]), waits=wa, inc=sv)
                agt = kb.emit("vector", lambda v: v.bn_aggr(
                    st3[:, 2:4], st3[:, 4:16]), waits=[b1t, b2t], inc=sv)
                lnt = kb.emit("scalar", lambda s_: s_.activation(
                    st3[:, 16:17], st3[:, 3:4], AF.Ln, bias=eps_col[:, :]),
                    waits=[agt], inc=ss)
                rst = kb.emit("scalar", lambda s_: s_.activation(
                    st3[:, 17:18], st3[:, 16:17], AF.Exp, scale=-0.5),
                    waits=[lnt], inc=ss)
                nmt = kb.emit("vector", lambda v: v.tensor_scalar(
                    st3[:, 18:19], st3[:, 2:3], st3[:, 17:18], -1.0,
                    ALU.mult, ALU.mult), waits=[rst], inc=sv)
                zt = kb.emit("vector", lambda v, av=av: v.tensor_scalar(
                    zbf[:, :], av, st3[:, 17:18], st3[:, 18:19],
                    ALU.mult, ALU.add),
                    waits=[nmt] + R3.waits("zb"), inc=sv)
                R3.use("at%d_%d" % (qbi, apar), [zt, b1t, b2t])
                rwp = R3.waits("pms") + prev_exp[0]
                mts = []
                for dc in range(8):
                    mts.append(kb.emit("tensor", lambda t, dc=dc: t.matmul(
                        pms[:, dc * 128:(dc + 1) * 128],
                        zbf[:, dc * 128:(dc + 1) * 128], ident[:, :],
                        start=True, stop=True),
                        waits=[zt, tid] + rwp,
                        inc=(st if dc == 7 else None)))
                mts = mts[-1:]
                R3.use("zb", mts)
                dstn = ap(nT, qbi * 128, [[8 * P3S, 128], [P3S, 8], [1, 128]])
                evn = kb.emit("vector", lambda v, dstn=dstn: v.tensor_copy(
                    dstn, ap(pms, 0, [[3072, 128], [128, 8], [1, 128]])),
                    waits=mts + R3.waits("nt%d" % qbi), inc=sv)
                R3.use("pms", [evn])
                prev_exp[0] = prev_exp[0] + [evn]
                wo_mms = []
                fvs = []
                for dh in range(2):
                    slot = ((qbi * 2 + dh) % 2) * 512
                    rw = R3.waits("pv%d" % (slot // 512))
                    rk = kb.emit("tensor", lambda t, slot=slot, dh=dh:
                                 t.matmul(ppv[:, slot:slot + 512],
                                          ones_row[:, :],
                                          borow[:, dh * 512:(dh + 1) * 512],
                                          start=True, stop=False),
                                 waits=[t_bop] + rw, inc=st)
                    last = rk
                    for dc in range(8):
                        last = kb.emit(
                            "tensor",
                            lambda t, slot=slot, dc=dc, dh=dh, qbi=qbi:
                            t.matmul(ppv[:, slot:slot + 512],
                                     nT[:, dc * P3S + qbi * 128:
                                        dc * P3S + qbi * 128 + 128],
                                     wo[:, dc * D + dh * 512:
                                        dc * D + (dh + 1) * 512],
                                     start=False, stop=(dc == 7)),
                            waits=[evn, wo_toks[dc]],
                            inc=(st if dc == 7 else None))
                    wo_mms.append(last)
                    rwo = [(od_y[qbi], od_y[qbi].n)]
                    fv0 = kb.emit("vector",
                                  lambda v, slot=slot, qbi=qbi, dh=dh,
                                  spar=spar:
                                  v.tensor_tensor(
                                      outb[:, qbi * D + dh * 512:
                                           qbi * D + (dh + 1) * 512],
                                      ppv[:, slot:slot + 512],
                                      partb[:, spar * NQB * D + qbi * D +
                                            dh * 512:spar * NQB * D +
                                            qbi * D + (dh + 1) * 512],
                                      ALU.add),
                                  waits=[last, p_tok[spar]] + rwo, inc=sv)
                    R3.use("pv%d" % (slot // 512), [fv0])
                    fv = kb.emit("vector",
                                 lambda v, qbi=qbi, dh=dh:
                                 v.tensor_tensor(
                                     outb[:, qbi * D + dh * 512:
                                          qbi * D + (dh + 1) * 512],
                                     outb[:, qbi * D + dh * 512:
                                          qbi * D + (dh + 1) * 512],
                                     xb3[:, qbi * D + dh * 512:
                                         qbi * D + (dh + 1) * 512],
                                     ALU.add),
                                 waits=[fv0, x_tok3[0]], inc=sv)
                    R3.use("xb3", R3.waits("xb3") + [fv])
                    fvs.append(fv)
                R3.use("nt%d" % qbi, wo_mms)
                R3.use("pb%d" % spar,
                       R3.waits("pb%d" % spar) + fvs if qbi else fvs)
                wr = kb.emit("sync", lambda s_, s3=s3, qbi=qbi: s_.dma_start(
                    y_out.ap()[s3 * P3S + qbi * 128:
                               s3 * P3S + (qbi + 1) * 128, :],
                    outb[:, qbi * D:(qbi + 1) * D]),
                    waits=fvs, inc=od_y[qbi], inc_k=16)
                outwr.append(wr)

        def emit_xload3(s3):
            x_tok3[0] = kb.emit("sync", lambda s_, s3=s3: s_.dma_start(
                ap(xb3, 0, [[NQB * D, 128], [D, NQB], [1, D]]),
                ap(x_in, s3 * P3S * D, [[D, 128], [128 * D, NQB], [1, D]])),
                waits=R3.waits("xb3"), inc=sx3, inc_k=16)

        emit_loads(0)
        pend = None
        lnwo_pend = [None]
        for s3 in range(N3):
            for h in range(NH):
                if h == NH // 2 and s3 + 1 < N3:
                    emit_loads(s3 + 1)
                if h == NH - 2:
                    emit_xload3(s3)
                done = pend
                et = emit_unit(s3, h, pend)
                pend = (s3, h, et)
                if done is not None and done[1] == NH - 1:
                    lnwo_pend[0] = done[0]
                elif lnwo_pend[0] is not None and h == 3:
                    emit_lnwo(lnwo_pend[0])
                    lnwo_pend[0] = None
        pv_sub_tail = pend
        for g in range(KC // 4):
            last_pv[0] = pv_sub(pv_sub_tail, g)
        pv_finish(pv_sub_tail, last_pv[0])
        if lnwo_pend[0] is not None:
            emit_lnwo(lnwo_pend[0])
        emit_lnwo(N3 - 1)
        kb.flush(blk)
    kb.barrier_reached()

    with nc.Block() as blk:
        kb.emit("sync", lambda s: s.nop(),
                waits=[(q, q.n) for q in od_y])
        kb.flush(blk)
    return nc


def kernel(**inputs):
    x = np.ascontiguousarray(np.asarray(inputs["L0_attn_hidden_states"],
                                        dtype=np.float32))
    B, S, _ = x.shape
    nc = build_nc(S=S)
    names = {
        "wq": "L0_attn_Wq", "bq": "L0_attn_bq", "wk": "L0_attn_Wk",
        "bk": "L0_attn_bk", "wv": "L0_attn_Wv", "bv": "L0_attn_bv",
        "wo": "L0_attn_Wo", "bo": "L0_attn_bo", "lq1": "L0_attn_lambda_q1",
        "lk1": "L0_attn_lambda_k1", "lq2": "L0_attn_lambda_q2",
        "lk2": "L0_attn_lambda_k2", "lnw": "L0_attn_ln_w",
        "lnb": "L0_attn_ln_b", "w1": "L0_ffn_W1", "b1": "L0_ffn_b1",
        "w2": "L0_ffn_W2", "b2": "L0_ffn_b2",
    }
    weights = {k: np.ascontiguousarray(np.asarray(inputs[v], np.float32))
               for k, v in names.items()}
    in_maps = []
    for b in range(B):
        m = {"x": np.ascontiguousarray(x[b])}
        m.update(weights)
        in_maps.append(m)
    res = run_bass_kernel_spmd(nc, in_maps, core_ids=list(range(B)))
    return np.stack([res.results[b]["y"] for b in range(B)], axis=0)
